# version 07
# baseline (speedup 1.0000x reference)
import numpy as np
import ml_dtypes
import concourse.bass as bass
import concourse.mybir as mybir
from concourse.bass_utils import run_bass_kernel_spmd
from contextlib import ExitStack

F32 = mybir.dt.float32
BF16 = mybir.dt.bfloat16
I32 = mybir.dt.int32
AF = mybir.ActivationFunctionType
ALU = mybir.AluOpType
BF = ml_dtypes.bfloat16

NCORE = 8
B, S, D = 2, 8192, 2048
TOK = B * S
TPC = TOK // NCORE
ND = D // 128
GT = 512
NG = TPC // GT
H0 = 8192
NH0T = H0 // 128
H1 = 5632
NH1T = H1 // 128
EPS_LN = 1e-5
W2SCALE = 0.5 / float(np.sqrt(H0))
DW = D // NCORE
ATTN_UPTO = 99
ATTN_DBG = False
ML_DBG = False
FORCE_SERIAL = False
SER_LN = False
SER_LN2 = False
SER_P1 = False
SER_P2 = False



class KB:

    def __init__(self):
        nc = bass.Bass("TRN2", target_bir_lowering=False, debug=False)
        self.nc = nc
        self.blk = nc.ctx.enter_context(nc.Block())
        self.evs = {n: self._sem("ev_" + n)
                    for n in ("vector", "scalar", "tensor", "gpsimd", "sync")}
        self.gld = self._sem("gld")
        self.sld = self._sem("sld")
        self.cc = self._sem("cc")
        self.adl = self._sem("adl")
        self._uid = 0
        self._wait_memo = {}

    def _sem(self, name):
        h = self.nc.ctx.enter_context(self.nc.semaphore(name))
        class Sem:
            pass
        s = Sem()
        s.h = h
        s.n = 0
        return s

    def uid(self, p):
        self._uid += 1
        return f"{p}_{self._uid}"

    def sb(self, name, shape, dtype, stk=None):
        ctx = stk if stk is not None else self.nc.ctx
        return ctx.enter_context(self.nc.sbuf_tensor(self.uid(name), shape, dtype))

    def psum(self, name, shape, dtype=F32, stk=None):
        ctx = stk if stk is not None else self.nc.ctx
        return ctx.enter_context(self.nc.psum_tensor(self.uid(name), shape, dtype))

    def dram(self, name, shape, dtype, kind="Internal", addr_space="Local"):
        return self.nc.dram_tensor(name, shape, dtype, kind=kind, addr_space=addr_space).ap()

    def _waits(self, eng, skip=None):
        memo = self._wait_memo.setdefault(eng.engine, {})
        sems = list(self.evs.values()) + [self.gld, self.sld, self.cc, self.adl]
        for s in sems:
            if s.n and s is not skip and memo.get(id(s), -1) < s.n:
                eng.wait_ge(s.h, s.n)
                memo[id(s)] = s.n

    def _expl_waits(self, eng, waits):
        memo = self._wait_memo.setdefault(eng.engine, {})
        for s, n in waits:
            if s is not None and n and memo.get(id(s), -1) < n:
                eng.wait_ge(s.h, n)
                memo[id(s)] = n

    def step(self, engname, f, waits="all"):
        sem = self.evs[engname]

        def body(eng):
            if waits == "all" or FORCE_SERIAL:
                self._waits(eng)
            else:
                self._expl_waits(eng, waits)
            last = f(eng)
            if last is not None:
                last.then_inc(sem.h, 1)
                sem.n += 1
        getattr(self.blk, engname)(body)
        return (sem, sem.n)

    def load(self, engname, pairs, waits="all"):
        sem = {"gpsimd": self.gld, "sync": self.sld}.get(engname)
        if sem is None:
            sem = self.adl

        def body(eng):
            if waits == "all" or FORCE_SERIAL:
                self._waits(eng)
            else:
                self._expl_waits(eng, waits)
            for out, in_ in pairs:
                eng.dma_start(out=out, in_=in_).then_inc(sem.h, 16)
                sem.n += 16
        getattr(self.blk, engname)(body)
        return sem.n

    def collective(self, kind, op, ins, outs):
        def body(gp):
            self._waits(gp)
            gp.collective_compute(kind, op, replica_groups=[list(range(NCORE))],
                                  ins=ins, outs=outs).then_inc(self.cc.h, 1)
            self.cc.n += 1
        self.blk.gpsimd(body)

    def barrier(self):
        for n in ("vector", "scalar", "tensor", "gpsimd", "sync"):
            def body(eng):
                self._waits(eng)
            getattr(self.blk, n)(body)

    def finish(self, engname="sync"):
        def body(eng):
            self._waits(eng)
        getattr(self.blk, engname)(body)



def emit_w2prep(k: KB, w2t, h128_in, out_view):
    FREE = 64 * DW
    stk0 = ExitStack()
    a_bf = k.sb("w2a", [128, FREE], BF16, stk=stk0)
    b_f = k.sb("w2b", [128, FREE], F32, stk=stk0)
    c_f = k.sb("w2c", [128, FREE], F32, stk=stk0)
    h128 = k.sb("h128", [128, 128], BF16, stk=stk0)
    pmm = k.psum("w2p", [128, 1024], stk=stk0)

    src = w2t.rearrange("(u v) d -> u v d", u=128)
    k.barrier()

    def ld(gp):
        pid = k.nc.partition_id([gp.engine])
        col0 = pid * DW
        gp.dma_start(out=a_bf[:].rearrange("p (v d) -> p v d", v=64),
                     in_=src[:, :, bass.ds(col0, DW)]).then_inc(k.gld.h, 16)
        k.gld.n += 16
        gp.dma_start(out=h128[:], in_=h128_in).then_inc(k.gld.h, 16)
        k.gld.n += 16
    k.blk.gpsimd(ld)

    NMM = FREE // 512
    for i in range(0, NMM, 2):
        def mm(t, i=i):
            t.matmul(pmm[:, 0:512], h128[:], a_bf[:, i * 512:(i + 1) * 512],
                     start=True, stop=True)
            return t.matmul(pmm[:, 512:1024], h128[:], a_bf[:, (i + 1) * 512:(i + 2) * 512],
                            start=True, stop=True)
        k.step("tensor", mm)

        def ev(s, i=i):
            s.copy(b_f[:, i * 512:(i + 1) * 512], pmm[:, 0:512])
            return s.copy(b_f[:, (i + 1) * 512:(i + 2) * 512], pmm[:, 512:1024])
        k.step("scalar", ev)

    def bfly(v):
        cur, nxt = b_f, c_f
        for stg in range(6):
            h = 1 << stg
            ngrp = 64 // (2 * h)
            av = cur[:].rearrange("p (g two h d) -> p g two h d", g=ngrp, two=2, h=h)
            nv = nxt[:].rearrange("p (g two h d) -> p g two h d", g=ngrp, two=2, h=h)
            v.tensor_add(nv[:, :, 0], av[:, :, 0], av[:, :, 1])
            v.tensor_sub(nv[:, :, 1], av[:, :, 0], av[:, :, 1])
            cur, nxt = nxt, cur
        return v.tensor_copy(a_bf[:], cur[:])
    k.step("vector", bfly)

    k.load("sync", [(out_view.rearrange("(u v) d -> u (v d)", u=128), a_bf[:])])
    stk0.close()


class LnEnv:
    def __init__(self, k: KB):
        self.xb = k.sb("lnxb", [128, 4 * GT], BF16)
        self.xsq = k.sb("lnxq", [128, 4 * GT], BF16)
        self.rows = k.sb("lnrw", [1, 2048], F32)
        self.rowsb = k.sb("lnrb", [1, 1024], BF16)
        self.scr = k.sb("lnsc", [128, GT], F32)
        self.onescol = k.sb("ones_c", [128, 1], BF16)
        self.onesrow = k.sb("ones_r", [1, 128], BF16)

        def init(v):
            v.memset(self.onescol[:], 1.0)
            return v.memset(self.onesrow[:], 1.0)
        k.step("vector", init)


ZMARK = (None, 0)


def emit_ln(k: KB, ln: LnEnv, xg, xlng, pA, pB):
    m_cast = {}
    m_sq = {}
    m_mm = {}
    for dt in range(ND):
        buf = dt % 4
        sl = slice(dt * GT, (dt + 1) * GT)
        bsl = slice(buf * GT, (buf + 1) * GT)

        def cst(v, sl=sl, bsl=bsl):
            return v.tensor_copy(ln.xb[:, bsl], xg[:, sl])
        m_cast[dt] = k.step("vector", cst,
                            waits="all" if (dt == 0 or SER_LN)
                            else [m_mm.get(dt - 4, ZMARK)])

        def sq(s, sl=sl, bsl=bsl):
            return s.activation(ln.xsq[:, bsl], xg[:, sl], AF.Square)
        m_sq[dt] = k.step("scalar", sq,
                          waits="all" if (dt == 0 or SER_LN)
                          else [m_mm.get(dt - 4, ZMARK)])

        def mm(t, dt=dt, bsl=bsl):
            t.matmul(pA[0:1, 0:GT], ln.onescol[:], ln.xb[:, bsl],
                     start=(dt == 0), stop=(dt == ND - 1))
            return t.matmul(pA[0:1, 512:512 + GT], ln.onescol[:], ln.xsq[:, bsl],
                            start=(dt == 0), stop=(dt == ND - 1))
        m_mm[dt] = k.step("tensor", mm,
                          waits="all" if SER_LN else [m_cast[dt], m_sq[dt]])

    def rows1(v):
        mu = ln.rows[0:1, 0:GT]
        msq = ln.rows[0:1, 512:512 + GT]
        var = ln.rows[0:1, 1024:1024 + GT]
        v.tensor_scalar_mul(mu, pA[0:1, 0:GT], 1.0 / D)
        v.tensor_scalar_mul(msq, pA[0:1, 512:512 + GT], 1.0 / D)
        v.tensor_mul(var, mu, mu)
        v.tensor_sub(var, msq, var)
        return v.tensor_scalar_add(var, var, EPS_LN)
    m = k.step("vector", rows1, waits="all" if SER_LN2 else [m_mm[ND - 1]])

    def rows2(s):
        return s.activation(ln.rows[0:1, 1536:1536 + GT], ln.rows[0:1, 1024:1024 + GT],
                            AF.Sqrt)
    m = k.step("scalar", rows2, waits="all" if SER_LN2 else [m])

    def rows3(v):
        v.reciprocal(ln.rows[0:1, 1024:1024 + GT], ln.rows[0:1, 1536:1536 + GT])
        v.tensor_copy(ln.rowsb[0:1, 0:GT], ln.rows[0:1, 0:GT])
        return v.tensor_copy(ln.rowsb[0:1, 512:512 + GT], ln.rows[0:1, 1024:1024 + GT])
    m = k.step("vector", rows3, waits="all" if SER_LN2 else [m])

    def bcast(t):
        t.matmul(pB[:, 0:GT], ln.onesrow[:], ln.rowsb[0:1, 0:GT], start=True, stop=True)
        return t.matmul(pB[:, 512:512 + GT], ln.onesrow[:], ln.rowsb[0:1, 512:512 + GT],
                        start=True, stop=True)
    m = k.step("tensor", bcast, waits="all" if SER_LN2 else [m])

    def apply(v):
        last = None
        for dt in range(ND):
            sl = slice(dt * GT, (dt + 1) * GT)
            v.tensor_sub(ln.scr[:], xg[:, sl], pB[:, 0:GT])
            last = v.tensor_mul(xlng[:, sl], ln.scr[:], pB[:, 512:512 + GT])
        return last
    return k.step("vector", apply, waits="all" if SER_LN2 else [m])


class Ffn0Env:
    def __init__(self, k: KB, stk=None):
        self.xg = k.sb("f0xg", [128, ND * GT], F32, stk=stk)
        self.xlng = k.sb("f0xl", [128, ND * GT], BF16, stk=stk)
        self.hid = k.sb("f0h", [128, NH0T * GT], BF16, stk=stk)
        self.w1b = k.sb("f0w1", [128, 2 * ND * 128], BF16, stk=stk)
        self.w3b = k.sb("f0w3", [128, 2 * ND * 128], BF16, stk=stk)
        self.w2b = k.sb("f0w2", [128, 2 * 8192], BF16, stk=stk)
        self.hsil = k.sb("f0hs", [128, 2 * GT], F32, stk=stk)
        self.xnew = k.sb("f0xn", [128, 8 * GT], F32, stk=stk)
        self.pA = k.psum("f0pA", [128, 1024], stk=stk)
        self.pB = k.psum("f0pB", [128, 1024], stk=stk)
        self.pO = k.psum("f0pO", [128, 2048], stk=stk)


def emit_ffn0(k: KB, ln: LnEnv, xin, xout, w1t, w3t, w2pairs, cast=True):
    stk0 = ExitStack()
    env = Ffn0Env(k, stk0)
    w1v = w1t.rearrange("(dt p) h -> p dt h", p=128)
    w3v = w3t.rearrange("(dt p) h -> p dt h", p=128)
    prev_mm1 = ZMARK
    k.barrier()

    st_after = 0
    prev_mm2_last = ZMARK
    m_epi = ZMARK
    for g in range(NG):
        gsl = slice(g * GT, (g + 1) * GT)
        k.load("sync", [(env.xg[:].rearrange("p (dt t) -> p dt t", dt=ND),
                         xin[:, gsl].rearrange("(dt p) t -> p dt t", p=128))])
        mark_apply = emit_ln(k, ln, env.xg, env.xlng, env.pA, env.pB)
        m_mm = {}
        m_mul = {}
        m_silu = {}
        wl_after = {}

        for ht in range(NH0T):
            buf = ht % 2
            hsl = slice(ht * 128, (ht + 1) * 128)
            wsl = slice(buf * 2048, buf * 2048 + 2048)
            lw = [m_mm.get(ht - 2, prev_mm1)]
            if cast:
                wl_after[ht] = k.load("gpsimd", [
                    (env.w1b[:, wsl].rearrange("p (dt h) -> p dt h", dt=ND),
                     w1v[:, :, hsl]),
                    (env.w3b[:, wsl].rearrange("p (dt h) -> p dt h", dt=ND),
                     w3v[:, :, hsl])], waits="all" if SER_P1 else lw)
            else:
                k.load("sync", [(env.w1b[:, wsl].rearrange("p (dt h) -> p dt h", dt=ND),
                                 w1v[:, :, hsl])], waits="all" if SER_P1 else lw)
                wl3 = k.load("scalar", [(env.w3b[:, wsl].rearrange("p (dt h) -> p dt h", dt=ND),
                                         w3v[:, :, hsl])], waits="all" if SER_P1 else lw)
                wl_after[ht] = (k.sld.n, wl3)

            bank = env.pA if buf == 0 else env.pB

            def mm1(t, buf=buf, bank=bank):
                for dt in range(ND):
                    xs = env.xlng[:, dt * GT:(dt + 1) * GT]
                    t.matmul(bank[:, 0:512], env.w1b[:, buf * 2048 + dt * 128:buf * 2048 + (dt + 1) * 128],
                             xs, start=(dt == 0), stop=(dt == ND - 1))
                for dt in range(ND):
                    xs = env.xlng[:, dt * GT:(dt + 1) * GT]
                    inst = t.matmul(bank[:, 512:1024], env.w3b[:, buf * 2048 + dt * 128:buf * 2048 + (dt + 1) * 128],
                                    xs, start=(dt == 0), stop=(dt == ND - 1))
                return inst
            if cast:
                ws = [(k.gld, wl_after[ht]), mark_apply,
                      m_mul.get(ht - 2, ZMARK), m_silu.get(ht - 2, ZMARK)]
            else:
                ws = [(k.sld, wl_after[ht][0]), (k.adl, wl_after[ht][1]), mark_apply,
                      m_mul.get(ht - 2, ZMARK), m_silu.get(ht - 2, ZMARK)]
            m_mm[ht] = k.step("tensor", mm1, waits="all" if SER_P1 else ws)

            def silu(s, buf=buf, bank=bank):
                return s.activation(env.hsil[:, buf * GT:(buf + 1) * GT],
                                    bank[:, 0:512], AF.Silu)
            m_silu[ht] = k.step("scalar", silu,
                                waits="all" if SER_P1
                                else [m_mm[ht], m_mul.get(ht - 2, ZMARK)])

            def mul(v, ht=ht, buf=buf, bank=bank):
                return v.tensor_mul(env.hid[:, ht * GT:(ht + 1) * GT],
                                    bank[:, 512:1024], env.hsil[:, buf * GT:(buf + 1) * GT])
            m_mul[ht] = k.step("vector", mul,
                               waits="all" if SER_P1 else [m_silu[ht], m_mm[ht]])

        prev_mm1 = m_mm[NH0T - 1]

        for rnd in range(2):
            banks = [env.pO[:, 0:512], env.pO[:, 512:1024], env.pO[:, 1024:1536],
                     env.pO[:, 1536:2048], env.pA[:, 0:512], env.pA[:, 512:1024],
                     env.pB[:, 0:512], env.pB[:, 512:1024]]
            m_mm2 = {}
            w2_after = {}
            for sb in range(8):
                buf = sb % 2
                w2view = env.w2b[:, buf * 8192:(buf + 1) * 8192]
                w2_after[sb] = k.load("sync", w2pairs(w2view, sb, rnd),
                                      waits="all" if SER_P2
                                      else [m_mm2.get(sb - 2, prev_mm2_last)])

                def mm2(t, sb=sb, buf=buf, banks=banks):
                    inst = None
                    for hh in range(8):
                        ht = sb * 8 + hh
                        for dtt in range(8):
                            inst = t.matmul(banks[dtt],
                                            env.w2b[:, buf * 8192 + hh * 1024 + dtt * 128:
                                                    buf * 8192 + hh * 1024 + (dtt + 1) * 128],
                                            env.hid[:, ht * GT:(ht + 1) * GT],
                                            start=(ht == 0), stop=(ht == NH0T - 1))
                    return inst
                ws = [(k.sld, w2_after[sb]), m_mul[NH0T - 1], m_epi]
                m_mm2[sb] = k.step("tensor", mm2, waits="all" if SER_P2 else ws)

            prev_mm2_last = m_mm2[7]
            em = ZMARK
            for dtt in range(8):
                dt = rnd * 8 + dtt

                def epi(v, dtt=dtt, dt=dt, banks=banks):
                    return v.scalar_tensor_tensor(
                        env.xnew[:, dtt * GT:(dtt + 1) * GT], banks[dtt], W2SCALE,
                        env.xg[:, dt * GT:(dt + 1) * GT], ALU.mult, ALU.add)
                em = k.step("vector", epi,
                            waits="all" if SER_P2
                            else [m_mm2[7], (k.sld, st_after)])
            m_epi = em
            st_after = k.load(
                "sync",
                [(xout[rnd * 1024:(rnd + 1) * 1024, gsl].rearrange("(dtt p) t -> p dtt t", p=128),
                  env.xnew[:].rearrange("p (dtt t) -> p dtt t", dtt=8))],
                waits="all" if SER_P2 else [m_epi])
    stk0.close()


class Ffn1Env:
    def __init__(self, k: KB, stk=None):
        self.xg = k.sb("f1xg", [128, ND * GT], F32, stk=stk)
        self.xlng = k.sb("f1xl", [128, ND * GT], BF16, stk=stk)
        self.hid = k.sb("f1h", [128, NH1T * GT], BF16, stk=stk)
        self.wgb = k.sb("f1wg", [128, 2 * ND * 128], BF16, stk=stk)
        self.wub = k.sb("f1wu", [128, 2 * ND * 128], BF16, stk=stk)
        self.wdb = k.sb("f1wd", [128, 2 * 8192], BF16, stk=stk)
        self.hsil = k.sb("f1hs", [128, 2 * GT], F32, stk=stk)
        self.xnew = k.sb("f1xn", [128, 8 * GT], F32, stk=stk)
        self.sgu = k.sb("f1sg", [128, 2 * NH1T], F32, stk=stk)
        self.sdh = k.sb("f1sd", [128, ND], F32, stk=stk)
        self.hrow = k.sb("f1hr", [1, GT], F32, stk=stk)
        self.hrowb = k.sb("f1hb", [1, GT], BF16, stk=stk)
        self.pA = k.psum("f1pA", [128, 1024], stk=stk)
        self.pB = k.psum("f1pB", [128, 1024], stk=stk)
        self.pO = k.psum("f1pO", [128, 2048], stk=stk)


def emit_ffn1(k: KB, ln: LnEnv, xin, xout, wgt, wut, wdt, sg, su, sd):
    stk0 = ExitStack()
    env = Ffn1Env(k, stk0)
    wgv = wgt.rearrange("(dt p) h -> p dt h", p=128)
    wuv = wut.rearrange("(dt p) h -> p dt h", p=128)
    NSB = (NH1T + 7) // 8
    prev_mm1 = ZMARK
    cast = wgt.dtype != BF16
    k.barrier()

    k.load("sync", [(env.sgu[:, 0:NH1T], sg), (env.sgu[:, NH1T:2 * NH1T], su),
                    (env.sdh[:], sd)])

    def halfsd(v):
        return v.tensor_scalar_mul(env.sdh[:], env.sdh[:], 0.5)
    k.step("vector", halfsd)

    st_after = 0
    prev_mm2_last = ZMARK
    m_epi = ZMARK
    for g in range(NG):
        gsl = slice(g * GT, (g + 1) * GT)
        k.load("sync", [(env.xg[:].rearrange("p (dt t) -> p dt t", dt=ND),
                         xin[:, gsl].rearrange("(dt p) t -> p dt t", p=128))])
        mark_apply = emit_ln(k, ln, env.xg, env.xlng, env.pA, env.pB)

        def xsum(t):
            inst = None
            for dt in range(ND):
                inst = t.matmul(env.pB[0:1, 0:GT], ln.onescol[:],
                                env.xlng[:, dt * GT:(dt + 1) * GT],
                                start=(dt == 0), stop=(dt == ND - 1))
            return inst
        m_xs = k.step("tensor", xsum, waits=[mark_apply])

        def xsrow(v):
            v.tensor_scalar_mul(env.hrow[:], env.pB[0:1, 0:GT], -128.0)
            return v.tensor_copy(env.hrowb[:], env.hrow[:])
        m_xsr = k.step("vector", xsrow, waits=[m_xs])

        m_mm = {}
        m_mul = {}
        m_silu = {}
        for ht in range(NH1T):
            buf = ht % 2
            hsl = slice(ht * 128, (ht + 1) * 128)
            wsl = slice(buf * 2048, buf * 2048 + 2048)
            lw = [m_mm.get(ht - 2, prev_mm1)]
            if cast:
                wl = k.load("gpsimd", [
                    (env.wgb[:, wsl].rearrange("p (dt h) -> p dt h", dt=ND),
                     wgv[:, :, hsl]),
                    (env.wub[:, wsl].rearrange("p (dt h) -> p dt h", dt=ND),
                     wuv[:, :, hsl])], waits=lw)
            else:
                wl_s = k.load("sync", [
                    (env.wgb[:, wsl].rearrange("p (dt h) -> p dt h", dt=ND),
                     wgv[:, :, hsl])], waits=lw)
                wl_a = k.load("scalar", [
                    (env.wub[:, wsl].rearrange("p (dt h) -> p dt h", dt=ND),
                     wuv[:, :, hsl])], waits=lw)

            bank = env.pA if buf == 0 else env.pB

            def mm1(t, buf=buf, bank=bank):
                t.matmul(bank[:, 0:512], ln.onesrow[:], env.hrowb[0:1, :],
                         start=True, stop=False)
                t.matmul(bank[:, 512:1024], ln.onesrow[:], env.hrowb[0:1, :],
                         start=True, stop=False)
                for dt in range(ND):
                    xs = env.xlng[:, dt * GT:(dt + 1) * GT]
                    t.matmul(bank[:, 0:512], env.wgb[:, buf * 2048 + dt * 128:buf * 2048 + (dt + 1) * 128],
                             xs, start=False, stop=(dt == ND - 1))
                for dt in range(ND):
                    xs = env.xlng[:, dt * GT:(dt + 1) * GT]
                    inst = t.matmul(bank[:, 512:1024], env.wub[:, buf * 2048 + dt * 128:buf * 2048 + (dt + 1) * 128],
                                    xs, start=False, stop=(dt == ND - 1))
                return inst
            if cast:
                ws = [(k.gld, wl), m_xsr, mark_apply,
                      m_mul.get(ht - 2, ZMARK), m_silu.get(ht - 2, ZMARK)]
            else:
                ws = [(k.sld, wl_s), (k.adl, wl_a), m_xsr, mark_apply,
                      m_mul.get(ht - 2, ZMARK), m_silu.get(ht - 2, ZMARK)]
            m_mm[ht] = k.step("tensor", mm1, waits=ws)

            def silu(s, ht=ht, buf=buf, bank=bank):
                return s.activation(env.hsil[:, buf * GT:(buf + 1) * GT],
                                    bank[:, 0:512], AF.Silu,
                                    scale=env.sgu[:, ht:ht + 1])
            m_silu[ht] = k.step("scalar", silu,
                                waits=[m_mm[ht], m_mul.get(ht - 2, ZMARK)])

            def mul(v, ht=ht, buf=buf, bank=bank):
                return v.scalar_tensor_tensor(env.hid[:, ht * GT:(ht + 1) * GT],
                                              bank[:, 512:1024],
                                              env.sgu[:, NH1T + ht:NH1T + ht + 1],
                                              env.hsil[:, buf * GT:(buf + 1) * GT],
                                              ALU.mult, ALU.mult)
            m_mul[ht] = k.step("vector", mul, waits=[m_silu[ht], m_mm[ht]])

        prev_mm1 = m_mm[NH1T - 1]

        def hsum(t):
            inst = None
            for ht in range(NH1T):
                inst = t.matmul(env.pB[0:1, 0:GT], ln.onescol[:],
                                env.hid[:, ht * GT:(ht + 1) * GT],
                                start=(ht == 0), stop=(ht == NH1T - 1))
            return inst
        m_hs = k.step("tensor", hsum, waits=[m_mul[NH1T - 1], m_xsr])

        def hrow(v):
            v.tensor_scalar_mul(env.hrow[:], env.pB[0:1, 0:GT], -128.0)
            return v.tensor_copy(env.hrowb[:], env.hrow[:])
        m_hr = k.step("vector", hrow, waits=[m_hs])

        for rnd in range(2):
            banks = [env.pO[:, 0:512], env.pO[:, 512:1024], env.pO[:, 1024:1536],
                     env.pO[:, 1536:2048], env.pA[:, 0:512], env.pA[:, 512:1024],
                     env.pB[:, 0:512], env.pB[:, 512:1024]]
            m_mm2 = {}
            for sb in range(NSB):
                buf = sb % 2
                nhh = min(8, NH1T - sb * 8)
                eng2 = "gpsimd" if cast else "scalar"
                wl2 = k.load(eng2, [
                    (env.wdb[:, buf * 8192:buf * 8192 + nhh * 1024].rearrange(
                        "p (hh o) -> p hh o", hh=nhh),
                     wdt[sb * 1024:sb * 1024 + nhh * 128,
                         rnd * 1024:(rnd + 1) * 1024].rearrange("(hh p) o -> p hh o", p=128))],
                    waits=[m_mm2.get(sb - 2, prev_mm2_last)])

                def mm2(t, sb=sb, buf=buf, nhh=nhh, banks=banks):
                    inst = None
                    if sb == 0:
                        for dtt in range(8):
                            inst = t.matmul(banks[dtt], ln.onesrow[:], env.hrowb[0:1, :],
                                            start=True, stop=False)
                    for hh in range(nhh):
                        ht = sb * 8 + hh
                        for dtt in range(8):
                            inst = t.matmul(banks[dtt],
                                            env.wdb[:, buf * 8192 + hh * 1024 + dtt * 128:
                                                    buf * 8192 + hh * 1024 + (dtt + 1) * 128],
                                            env.hid[:, ht * GT:(ht + 1) * GT],
                                            start=False, stop=(ht == NH1T - 1))
                    return inst
                ws = [(k.gld if cast else k.adl, wl2), m_hr, m_mul[NH1T - 1], m_epi]
                m_mm2[sb] = k.step("tensor", mm2, waits=ws)

            prev_mm2_last = m_mm2[NSB - 1]
            em = ZMARK
            for dtt in range(8):
                dt = rnd * 8 + dtt

                def epi(v, dtt=dtt, dt=dt, banks=banks):
                    return v.scalar_tensor_tensor(
                        env.xnew[:, dtt * GT:(dtt + 1) * GT], banks[dtt],
                        env.sdh[:, dt:dt + 1],
                        env.xg[:, dt * GT:(dt + 1) * GT], ALU.mult, ALU.add)
                em = k.step("vector", epi,
                            waits=[m_mm2[NSB - 1], (k.sld, st_after)])
            m_epi = em
            st_after = k.load(
                "sync",
                [(xout[rnd * 1024:(rnd + 1) * 1024, gsl].rearrange("(dtt p) t -> p dtt t", p=128),
                  env.xnew[:].rearrange("p (dtt t) -> p dtt t", dtt=8))],
                waits=[m_epi])
    stk0.close()


def emit_attn(k: KB, ln: LnEnv, xin, xout, wqkvt, woutt, i128bf, trimask, contrib, ag, contrib_o, ago):
    nc = k.nc
    NHH = 16

    with ExitStack() as stk:
        xg = k.sb("atxg", [128, ND * GT], F32, stk=stk)
        xlng = k.sb("atxl", [128, ND * GT], BF16, stk=stk)
        wblk = k.sb("atwb", [128, 2 * ND * 512], BF16, stk=stk)
        qko = k.sb("atqk", [128, 2 * 512], BF16, stk=stk)
        pA = k.psum("atpA", [128, 1024], stk=stk)
        pB = k.psum("atpB", [128, 1024], stk=stk)
        wqv = wqkvt.rearrange("(dt p) o -> p dt o", p=128)
        k.barrier()
        m_mm = {}
        m_evc = {}
        st_q = {}

        for g in range(NG):
            gsl = slice(g * GT, (g + 1) * GT)
            k.load("sync", [(xg[:].rearrange("p (dt t) -> p dt t", dt=ND),
                             xin[:, gsl].rearrange("(dt p) t -> p dt t", p=128))])
            mark_apply = emit_ln(k, ln, xg, xlng, pA, pB)

            for ot in range(32):
                buf = ot % 2
                osl = slice(ot * 128, (ot + 1) * 128)
                wl = k.load("gpsimd",
                            [(wblk[:, buf * 2048:buf * 2048 + 2048].rearrange(
                                "p (dt h) -> p dt h", dt=ND), wqv[:, :, osl])],
                            waits=[m_mm.get(("qk", g, ot - 2),
                                            m_mm.get(("v", g - 1, 2, 3), ZMARK))])
                bank = pA[:, 0:512] if buf == 0 else pA[:, 512:1024]

                def mmqk(t, buf=buf, bank=bank):
                    inst = None
                    for dt in range(ND):
                        inst = t.matmul(bank, wblk[:, buf * 2048 + dt * 128:buf * 2048 + (dt + 1) * 128],
                                        xlng[:, dt * GT:(dt + 1) * GT],
                                        start=(dt == 0), stop=(dt == ND - 1))
                    return inst
                m_mm[("qk", g, ot)] = k.step(
                    "tensor", mmqk,
                    waits=[(k.gld, wl), mark_apply,
                           m_evc.get(("qk", g, ot - 2), m_evc.get(("last", g - 1), ZMARK))])

                def ev(v, buf=buf, bank=bank):
                    return v.tensor_copy(qko[:, buf * 512:(buf + 1) * 512], bank)
                m_evc[("qk", g, ot)] = k.step(
                    "vector", ev,
                    waits=[m_mm[("qk", g, ot)],
                           (k.sld, st_q.get(("qk", g, ot - 2),
                                            st_q.get(("v", g - 1, 3, 2 + (ot % 2)), 0)))])
                st_q[("qk", g, ot)] = k.load(
                    "sync", [(contrib[ot * 128:(ot + 1) * 128, gsl],
                              qko[:, buf * 512:(buf + 1) * 512])],
                    waits=[m_evc[("qk", g, ot)]])

            prev_ev = m_evc[("qk", g, 31)]
            for vb in range(4):
                buf = vb % 2
                vsl = slice(4096 + vb * 512, 4096 + (vb + 1) * 512)
                wl = k.load("gpsimd",
                            [(wblk[:, buf * 8192:(buf + 1) * 8192].rearrange(
                                "p (dt h) -> p dt h", dt=ND), wqv[:, :, vsl])],
                            waits=[m_mm.get(("v", g, vb - 2, 3),
                                            m_mm.get(("qk", g, 31), ZMARK))])
                for ttl in range(4):
                    pbuf = ttl % 2
                    bank = pB[:, 0:512] if pbuf == 0 else pB[:, 512:1024]

                    def mmv(t, ttl=ttl, buf=buf, bank=bank):
                        inst = None
                        for dt in range(ND):
                            inst = t.matmul(bank,
                                            xlng[:, dt * GT + ttl * 128:dt * GT + (ttl + 1) * 128],
                                            wblk[:, buf * 8192 + dt * 512:buf * 8192 + (dt + 1) * 512],
                                            start=(dt == 0), stop=(dt == ND - 1))
                        return inst
                    m_mm[("v", g, vb, ttl)] = k.step(
                        "tensor", mmv,
                        waits=[(k.gld, wl), mark_apply,
                               m_evc.get(("v", g, vb, ttl - 2),
                                         m_evc.get(("v", g, vb - 1, 2 + ttl), prev_ev))])

                    def ev(v, buf2=(vb * 4 + ttl) % 2, bank=bank):
                        return v.tensor_copy(qko[:, buf2 * 512:(buf2 + 1) * 512], bank)
                    m_evc[("v", g, vb, ttl)] = k.step(
                        "vector", ev,
                        waits=[m_mm[("v", g, vb, ttl)],
                               (k.sld, st_q.get(("v", g, vb, ttl - 2),
                                                st_q.get(("v", g, vb - 1, 2 + ttl),
                                                         st_q.get(("qk", g, 30 + ((vb * 4 + ttl) % 2)), 0))))])
                    st_q[("v", g, vb, ttl)] = k.load(
                        "sync", [(contrib[4096 + g * GT + ttl * 128:
                                          4096 + g * GT + (ttl + 1) * 128,
                                          vb * 512:(vb + 1) * 512],
                                  qko[:, ((vb * 4 + ttl) % 2) * 512:(((vb * 4 + ttl) % 2) + 1) * 512])],
                        waits=[m_evc[("v", g, vb, ttl)]])
            m_evc[("last", g)] = m_evc[("v", g, 3, 3)]

    if ATTN_UPTO < 1:
        return
    if ATTN_DBG:
        dbg_contrib = k.dram("dbg_contrib", [3 * D, TPC], BF16, kind="ExternalOutput")
        k.load("sync", [(dbg_contrib, contrib)])
    k.collective("AllGather", ALU.bypass, [contrib], [ag])

    kc = k.dram(k.uid("at_kc"), [2048, 2048], BF16)
    qc = k.dram(k.uid("at_qc"), [2048, 2048], BF16)
    vc = k.dram(k.uid("at_vc"), [TOK, 256], BF16)

    def canon(gp):
        pid = k.nc.partition_id([gp.engine])
        agk = ag.rearrange("(brl sec hh p) t -> brl sec hh p t",
                           brl=8, sec=3, hh=16)
        qdst = qc.rearrange("(brl sp) t -> brl sp t", brl=8)
        kdst = kc.rearrange("(brl sp) t -> brl sp t", brl=8)
        src_q = agk[:, 0, bass.ds(pid * 2, 2)].rearrange("brl s p t -> brl (s p) t")
        src_k = agk[:, 1, bass.ds(pid * 2, 2)].rearrange("brl s p t -> brl (s p) t")
        gp.dma_start(out=qdst, in_=src_q).then_inc(k.gld.h, 16)
        k.gld.n += 16
        gp.dma_start(out=kdst, in_=src_k).then_inc(k.gld.h, 16)
        k.gld.n += 16
        if ATTN_UPTO >= 2.5:
            agv = ag.rearrange("(bb rlt sec tok) (hp c) -> bb rlt sec tok hp c",
                               bb=2, rlt=4, sec=3, hp=8)
            vdst = vc.rearrange("(bb rlt tok) c -> bb rlt tok c", bb=2, rlt=4)
            for bb in range(2):
                for rlt in range(4):
                    gp.dma_start(out=vdst[bb, rlt],
                                 in_=agv[bb, rlt, 2, :, bass.ds(pid, 1), :].squeeze(axis=1)
                                 ).then_inc(k.gld.h, 16)
                    k.gld.n += 16
    if ATTN_UPTO < 2:
        return
    k.step("gpsimd", canon)

    if ATTN_DBG:
        dbg_kc = k.dram("dbg_kc", [2048, 2048], BF16, kind="ExternalOutput")
        dbg_qc = k.dram("dbg_qc", [2048, 2048], BF16, kind="ExternalOutput")
        dbg_vc = k.dram("dbg_vc", [TOK, 256], BF16, kind="ExternalOutput")
        k.load("sync", [(dbg_kc, kc), (dbg_qc, qc), (dbg_vc, vc)])
    if ATTN_UPTO < 3:
        return
    with ExitStack() as stk:
        Kf = k.sb("atK", [128, S], BF16, stk=stk)
        Qf = k.sb("atQ", [128, S], BF16, stk=stk)
        vt = k.sb("atV", [128, 64 * 129], BF16, stk=stk)
        ptb = k.sb("atP", [128, 2048], BF16, stk=stk)
        sq = k.sb("atsq", [128, 512], BF16, stk=stk)
        srow = k.sb("atsr", [1, 8], F32, stk=stk)
        chm = k.sb("atcm", [1, 32], F32, stk=stk)
        srowb = k.sb("atsb", [1, 1], BF16, stk=stk)
        negsh = k.sb("atns", [128, 1], F32, stk=stk)
        scl128 = k.sb("atsc", [128, 1], F32, stk=stk)
        ob = k.sb("atob", [128, 256], BF16, stk=stk)
        rl = k.sb("atrl", [128, 8], F32, stk=stk)
        pS = k.psum("atpS", [128, 2048], stk=stk)
        pacc = k.psum("atpa", [128, 1024], stk=stk)
        pacc2 = k.psum("atp2", [128, 512], stk=stk)
        pn = k.psum("atpn", [128, 512], stk=stk)
        accs = [pacc[:, j * 132:j * 132 + 129] for j in range(3)] + \
               [pacc[:, 512 + j * 132:512 + j * 132 + 129] for j in range(3)] + \
               [pacc2[:, j * 132:j * 132 + 129] for j in range(2)]
        vtv = vt[:].rearrange("p (c w) -> p c w", w=129)

        nslots = 1 if ATTN_UPTO < 5 else 4
        for slot in range(nslots):
            b = slot // 2
            s = slot % 2
            vsl = vc[b * 8192:(b + 1) * 8192, s * 128:(s + 1) * 128].rearrange(
                "(c p) w -> p c w", p=128)
            qkp = [(Qf[:, rl * 2048:(rl + 1) * 2048],
                    qc[b * 1024 + rl * 256 + s * 128:b * 1024 + rl * 256 + (s + 1) * 128, :])
                   for rl in range(4)]
            qkp += [(Kf[:, rl * 2048:(rl + 1) * 2048],
                     kc[b * 1024 + rl * 256 + s * 128:b * 1024 + rl * 256 + (s + 1) * 128, :])
                    for rl in range(4)]
            qkp.append((vtv[:, :, 0:128], vsl))
            k.load("sync", qkp)

            def onesv(v):
                v.memset(vtv[:, :, 128:129], 1.0)
                v.memset(scl128[:], 1.0 / float(np.sqrt(128.0)))
                return v.memset(srow[:], 0.0)
            k.step("vector", onesv)

            for which, buf in ((0, Kf), (1, Qf)):
                for i in range(16):
                    def sqr(s, buf=buf, i=i):
                        return s.activation(sq[:], buf[:, i * 512:(i + 1) * 512], AF.Square)
                    k.step("scalar", sqr)

                    def nmm(t):
                        return t.matmul(pn[0:1, 0:512], ln.onescol[:], sq[:],
                                        start=True, stop=True)
                    k.step("tensor", nmm)

                    def nmax(v, which=which, i=i):
                        return v.reduce_max(chm[0:1, which * 16 + i:which * 16 + i + 1],
                                            pn[0:1, 0:512], mybir.AxisListType.X)
                    k.step("vector", nmax)

            def nfin(v):
                v.reduce_max(srow[0:1, 0:1], chm[0:1, 0:16], mybir.AxisListType.X)
                return v.reduce_max(srow[0:1, 1:2], chm[0:1, 16:32], mybir.AxisListType.X)
            k.step("vector", nfin)

            def shrow(s):
                s.activation(srow[0:1, 2:3], srow[0:1, 0:1], AF.Sqrt)
                return s.activation(srow[0:1, 3:4], srow[0:1, 1:2], AF.Sqrt)
            k.step("scalar", shrow)

            def shrow2a(v):
                return v.tensor_mul(srow[0:1, 5:6], srow[0:1, 2:3], srow[0:1, 3:4])
            k.step("vector", shrow2a)

            def shrow2b(v):
                return v.tensor_scalar(srowb[:], srow[0:1, 5:6],
                                       1.0 / float(np.sqrt(128.0)), None, ALU.mult)
            k.step("vector", shrow2b)

            def shb(t):
                return t.matmul(pn[:, 0:1], ln.onesrow[:], srowb[0:1, 0:1],
                                start=True, stop=True)
            k.step("tensor", shb)

            def shneg(v):
                return v.tensor_scalar_mul(negsh[:], pn[:, 0:1], -1.0)
            k.step("vector", shneg)

            if ATTN_DBG:
                dbg_sh = k.dram(k.uid("dbg_sh"), [128, 1], F32, kind="ExternalOutput")
                dbg_srow = k.dram(k.uid("dbg_srow"), [1, 8], F32, kind="ExternalOutput")
                k.load("sync", [(dbg_sh, negsh[:]), (dbg_srow, srow[:])])

            if ATTN_UPTO < 4:
                continue
            m_st = {}
            m_expo = {}
            m_pv = {}
            m_fin1 = {}
            m_fin2 = {}
            ob_st = {}
            m_fin_last = ZMARK
            ci = 0
            for qg in range(8):
                nch = 8 * qg + 8
                chunks = []
                for c in range(nch):
                    jlo = max(0, c - 8 * qg)
                    chunks.append((ci + c, c, jlo, 1024 - jlo * 128,
                                   qg * 1024 + jlo * 128))

                def emit_st(t, c, W, q0, jlo, par, qg=qg):
                    inst = None
                    diag = c >= 8 * qg
                    off = 0
                    base = par * 1024
                    while off < W:
                        seg = min(512, W - off)
                        inst = t.matmul(pS[:, base + off:base + off + seg],
                                        Kf[:, c * 128:(c + 1) * 128],
                                        Qf[:, q0 + off:q0 + off + seg],
                                        start=True,
                                        stop=not (diag and off == 0))
                        off += seg
                    if diag:
                        inst = t.matmul(pS[:, base:base + 128], i128bf, trimask,
                                        start=False, stop=True)
                    return inst

                for (ii, c, jlo, W, q0) in chunks:
                    par = ii % 2
                    if c == 0:
                        m_st[ii] = k.step(
                            "tensor",
                            lambda t, c=c, W=W, q0=q0, jlo=jlo, par=par: emit_st(t, c, W, q0, jlo, par),
                            waits=[m_expo.get(ii - 2, ZMARK), m_fin_last])

                    def expo(s, W=W, par=par):
                        return s.activation(ptb[:, par * 1024:par * 1024 + W],
                                            pS[:, par * 1024:par * 1024 + W], AF.Exp,
                                            bias=negsh[:], scale=scl128[:])
                    m_expo[ii] = k.step("scalar", expo,
                                        waits=[m_st[ii], m_pv.get(ii - 2, ZMARK)])

                    if c + 1 < nch:
                        (ii2, c2, jlo2, W2, q02) = chunks[c + 1]
                        m_st[ii2] = k.step(
                            "tensor",
                            lambda t, c=c2, W=W2, q0=q02, jlo=jlo2, par=ii2 % 2: emit_st(t, c, W, q0, jlo, par),
                            waits=[m_expo.get(ii2 - 2, ZMARK), m_fin_last])

                    def pv(t, c=c, jlo=jlo, qg=qg, par=par):
                        inst = None
                        for j in range(jlo, 8):
                            inst = t.matmul(accs[j],
                                            ptb[:, par * 1024 + (j - jlo) * 128:par * 1024 + (j - jlo + 1) * 128],
                                            vtv[:, c, :],
                                            start=(c == 0 and j in (0, 3, 6)),
                                            stop=(c == 8 * qg + j),
                                            skip_group_check=True)
                        return inst
                    m_pv[ii] = k.step("tensor", pv,
                                      waits=[m_expo[ii], m_fin_last])
                ci += nch

                for j in range(8):
                    def fin1(v, j=j):
                        return v.reciprocal(rl[:, j:j + 1], accs[j][:, 128:129])
                    m_fin1[j] = k.step("vector", fin1,
                                       waits=[m_pv[ci - 1], m_fin2.get(j - 2, ZMARK)])

                    def fin2(s, j=j):
                        return s.activation(ob[:, (j % 2) * 128:(j % 2) * 128 + 128],
                                            accs[j][:, 0:128], AF.Copy,
                                            scale=rl[:, j:j + 1])
                    m_fin2[j] = k.step("scalar", fin2,
                                       waits=[m_fin1[j], (k.sld, ob_st.get(j - 2, 0))])
                    ob_st[j] = k.load(
                        "sync",
                        [(contrib_o[slot * S + (qg * 8 + j) * 128:
                                    slot * S + (qg * 8 + j + 1) * 128, :],
                          ob[:, (j % 2) * 128:(j % 2) * 128 + 128])],
                        waits=[m_fin2[j]])
                m_fin_last = m_fin2[7]

    if ATTN_DBG:
        dbg_co = k.dram("dbg_co", [4 * S, 128], BF16, kind="ExternalOutput")
        k.load("sync", [(dbg_co, contrib_o)])
    if ATTN_UPTO < 6:
        return
    k.collective("AllGather", ALU.bypass, [contrib_o], [ago])

    with ExitStack() as stk:
        o_fm = k.sb("atof", [128, 16 * TPC], BF16, stk=stk)
        with ExitStack() as stk2:
            o_tm = k.sb("atot", [128, 16 * 128 * 16], BF16, stk=stk2)
            ptr = k.psum("atpt", [128, 512], BF16, stk=stk2)
            otv = o_tm[:].rearrange("p (tt v) -> p tt v", tt=16)

            oc = k.dram(k.uid("at_oc"), [16 * TPC, 128], BF16)

            def ocanon(gp):
                pid = nc.partition_id([gp.engine])
                agov = ago.rearrange("(r bb e q) w -> r bb e (q w)", r=8, bb=2, e=2)
                odst = oc.rearrange("(r e q) w -> r e (q w)", r=8, e=2)
                for e in range(2):
                    src = agov[:, bass.ds(pid // 4, 1), e,
                               bass.ds((pid % 4) * (2048 * 128), 2048 * 128)]
                    gp.dma_start(out=odst[:, e],
                                 in_=src.squeeze(axis=1)).then_inc(k.gld.h, 16)
                    k.gld.n += 16
            k.step("gpsimd", ocanon)

            k.load("sync", [(otv[:, :, h * 128:(h + 1) * 128],
                             oc[h * TPC:(h + 1) * TPC, :].rearrange("(tt p) v -> p tt v", p=128))
                            for h in range(16)])

            for vt_i in range(16):
                for tq in range(4):
                    def tr(t, vt_i=vt_i, tq=tq):
                        inst = None
                        for u in range(4):
                            tt = tq * 4 + u
                            inst = t.transpose(ptr[:, u * 128:(u + 1) * 128],
                                               otv[:, tt, vt_i * 128:(vt_i + 1) * 128],
                                               i128bf)
                        return inst
                    k.step("tensor", tr)

                    def ev(v, vt_i=vt_i, tq=tq):
                        return v.tensor_copy(
                            o_fm[:, vt_i * TPC + tq * 512:vt_i * TPC + (tq + 1) * 512],
                            ptr[:])
                    k.step("vector", ev)

        xg = k.sb("atxg2", [128, ND * GT], F32, stk=stk)
        xnew = k.sb("atxn", [128, 2 * GT], F32, stk=stk)
        wob = k.sb("atwo", [128, 16 * 1024], BF16, stk=stk)
        pA = k.psum("atpA2", [128, 1024], stk=stk)
        pB = k.psum("atpB2", [128, 1024], stk=stk)
        pO = k.psum("atpO2", [128, 2048], stk=stk)
        wov = woutt.rearrange("(vt p) o -> p vt o", p=128)

        for g in range(NG):
            gsl = slice(g * GT, (g + 1) * GT)
            k.load("sync", [(xg[:, dt * GT:(dt + 1) * GT],
                             xin[dt * 128:(dt + 1) * 128, gsl]) for dt in range(ND)])
            for rnd in range(2):
                osl = slice(rnd * 1024, (rnd + 1) * 1024)
                k.load("gpsimd", [(wob[:].rearrange("p (vt o) -> p vt o", vt=16),
                                   wov[:, :, osl])])
                banks = [pO[:, 0:512], pO[:, 512:1024], pO[:, 1024:1536],
                         pO[:, 1536:2048], pA[:, 0:512], pA[:, 512:1024],
                         pB[:, 0:512], pB[:, 512:1024]]

                def mmo(t, g=g, banks=banks):
                    inst = None
                    for vt_i in range(16):
                        for dtt in range(8):
                            inst = t.matmul(banks[dtt],
                                            wob[:, vt_i * 1024 + dtt * 128:vt_i * 1024 + (dtt + 1) * 128],
                                            o_fm[:, vt_i * TPC + g * GT:vt_i * TPC + (g + 1) * GT],
                                            start=(vt_i == 0), stop=(vt_i == 15))
                    return inst
                k.step("tensor", mmo)

                for dtt in range(8):
                    dt = rnd * 8 + dtt
                    buf = dtt % 2

                    def epi(v, dtt=dtt, dt=dt, buf=buf, banks=banks):
                        return v.scalar_tensor_tensor(
                            xnew[:, buf * GT:(buf + 1) * GT], banks[dtt], 1.0,
                            xg[:, dt * GT:(dt + 1) * GT], ALU.mult, ALU.add)
                    k.step("vector", epi)
                    k.load("sync", [(xout[dt * 128:(dt + 1) * 128, gsl],
                                     xnew[:, buf * GT:(buf + 1) * GT])])


def emit_outproj(k: KB, hc_dram, hccols, xin, xout, woutt, i128bf, gate_dram=None):
    nhb = D // hccols
    with ExitStack() as stk:
        o_fm = k.sb("opof", [128, 16 * TPC], BF16, stk=stk)
        m_ev_last = ZMARK
        with ExitStack() as stk2:
            o_tm = k.sb("opot", [128, 16 * D], BF16, stk=stk2)
            ptr1 = k.psum("oppt1", [128, 512], BF16, stk=stk2)
            ptr2 = k.psum("oppt2", [128, 512], BF16, stk=stk2)
            otv = o_tm[:].rearrange("p (tt v) -> p tt v", tt=16)

            k.load("sync", [(otv[:, :, h * hccols:(h + 1) * hccols],
                             hc_dram[h * TPC:(h + 1) * TPC, :].rearrange(
                                 "(tt p) v -> p tt v", p=128))
                            for h in range(nhb)])
            m_gate = ZMARK
            if gate_dram is not None:
                gt = k.sb("opgt", [128, 2 * D], BF16, stk=stk2)
                gsrc = gate_dram.rearrange("(tt p) v -> tt p v", p=128)
                m_gmul = {}
                gl_after = {}
                for tt in range(16):
                    buf = tt % 2
                    gl_after[tt] = k.load(
                        "sync", [(gt[:, buf * D:(buf + 1) * D], gsrc[tt])],
                        waits="all" if tt == 0 else [m_gmul.get(tt - 2, ZMARK)])

                    def gmul(v, tt=tt, buf=buf):
                        return v.tensor_mul(otv[:, tt, :], otv[:, tt, :],
                                            gt[:, buf * D:(buf + 1) * D])
                    m_gmul[tt] = k.step("vector", gmul,
                                        waits=[(k.sld, gl_after[tt])])
                m_gate = m_gmul[15]
            else:
                k.barrier()

            m_tr = {}
            m_evc = {}
            i = 0
            for vt_i in range(16):
                for tq in range(4):
                    par = i % 2
                    ptr = ptr1 if par == 0 else ptr2

                    def tr(t, vt_i=vt_i, tq=tq, ptr=ptr):
                        inst = None
                        for u in range(4):
                            tt = tq * 4 + u
                            inst = t.transpose(ptr[:, u * 128:(u + 1) * 128],
                                               otv[:, tt, vt_i * 128:(vt_i + 1) * 128],
                                               i128bf)
                        return inst
                    m_tr[i] = k.step("tensor", tr,
                                     waits=[m_gate, m_evc.get(i - 2, ZMARK)])

                    def ev(v, vt_i=vt_i, tq=tq, ptr=ptr):
                        return v.tensor_copy(
                            o_fm[:, vt_i * TPC + tq * 512:vt_i * TPC + (tq + 1) * 512],
                            ptr[:])
                    m_evc[i] = k.step("vector", ev, waits=[m_tr[i]])
                    i += 1
            m_ev_last = m_evc[i - 1]

        xg = k.sb("opxg", [128, ND * GT], F32, stk=stk)
        xnew = k.sb("opxn", [128, 8 * GT], F32, stk=stk)
        wob = k.sb("opwo", [128, 16 * 1024], BF16, stk=stk)
        pA = k.psum("oppA", [128, 1024], stk=stk)
        pB = k.psum("oppB", [128, 1024], stk=stk)
        pO = k.psum("oppO", [128, 2048], stk=stk)
        wov = woutt.rearrange("(vt p) o -> p vt o", p=128)
        m_epi = ZMARK
        m_mmq = {}
        st_after = 0
        xl_after = 0

        for g in range(NG):
            gsl = slice(g * GT, (g + 1) * GT)
            xl_after = k.load(
                "sync",
                [(xg[:].rearrange("p (dt t) -> p dt t", dt=ND),
                  xin[:, gsl].rearrange("(dt p) t -> p dt t", p=128))],
                waits=[m_epi] if g else "all")
            for rnd in range(2):
                osl = slice(rnd * 1024, (rnd + 1) * 1024)
                banks = [pO[:, 0:512], pO[:, 512:1024], pO[:, 1024:1536],
                         pO[:, 1536:2048], pA[:, 0:512], pA[:, 512:1024],
                         pB[:, 0:512], pB[:, 512:1024]]
                for q in range(4):
                    wlq = k.load("gpsimd",
                                 [(wob[:, q * 4096:(q + 1) * 4096].rearrange(
                                     "p (vt o) -> p vt o", vt=4),
                                   wov[:, 4 * q:4 * q + 4, osl])],
                                 waits=[m_mmq.get(q, ZMARK), m_ev_last])

                    def mmo(t, q=q, banks=banks):
                        inst = None
                        for vv in range(4):
                            vt_i = 4 * q + vv
                            for dtt in range(8):
                                inst = t.matmul(banks[dtt],
                                                wob[:, q * 4096 + vv * 1024 + dtt * 128:
                                                    q * 4096 + vv * 1024 + (dtt + 1) * 128],
                                                o_fm[:, vt_i * TPC + g * GT:vt_i * TPC + (g + 1) * GT],
                                                start=(vt_i == 0), stop=(vt_i == 15))
                            vt_i += 1
                        return inst
                    m_mmq[q] = k.step("tensor", mmo,
                                      waits=[(k.gld, wlq), m_ev_last, m_epi,
                                             (k.sld, xl_after)])

                em = ZMARK
                for dtt in range(8):
                    dt = rnd * 8 + dtt

                    def epi(v, dtt=dtt, dt=dt, banks=banks):
                        return v.scalar_tensor_tensor(
                            xnew[:, dtt * GT:(dtt + 1) * GT], banks[dtt], 1.0,
                            xg[:, dt * GT:(dt + 1) * GT], ALU.mult, ALU.add)
                    em = k.step("vector", epi,
                                waits=[m_mmq[3], (k.sld, st_after)])
                m_epi = em
                st_after = k.load(
                    "sync",
                    [(xout[rnd * 1024:(rnd + 1) * 1024, gsl].rearrange("(dtt p) t -> p dtt t", p=128),
                      xnew[:].rearrange("p (dtt t) -> p dtt t", dtt=8))],
                    waits=[m_epi])
    stk0_close = None


EPS_ML = 1e-6
LNS128 = float(np.log(1.0 / np.sqrt(128.0)))


def emit_mlstm(k: KB, ln: LnEnv, xin, xout, wqkT, wvT, wogT, wifT, ifbias, gmt_in,
               woutt, i128bf, i128f, trimp):
    nc = k.nc
    contrib = k.dram("ml_contrib", [4096, TPC], BF16)
    ag = k.dram("ml_ag", [NCORE * 4096, TPC], BF16, addr_space="Shared")
    contrib_if = k.dram("ml_cif", [16, TPC], F32)
    agif = k.dram("ml_agif", [NCORE * 16, TPC], F32, addr_space="Shared")
    contrib_h = k.dram("ml_ch", [2 * S, 256], BF16)
    agh = k.dram("ml_agh", [NCORE * 2 * S, 256], BF16, addr_space="Shared")
    og_l = k.dram("ml_og", [TPC, D], BF16)

    with ExitStack() as stk:
        xg = k.sb("mlxg", [128, ND * GT], F32, stk=stk)
        xlng = k.sb("mlxl", [128, ND * GT], BF16, stk=stk)
        wblk = k.sb("mlwb", [128, 2 * ND * 512], BF16, stk=stk)
        wifb = k.sb("mlwi", [128, ND * 16], BF16, stk=stk)
        ifbt = k.sb("mlib", [128, 16], F32, stk=stk)
        s15 = k.sb("mls15", [128, 1], F32, stk=stk)
        qko = k.sb("mlqk", [128, 2 * 512], BF16, stk=stk)
        sc1 = k.sb("mlsc", [128, 16], F32, stk=stk)
        ifr = k.sb("mlifr", [16, 128], F32, stk=stk)
        pA = k.psum("mlpA", [128, 1024], stk=stk)
        pB = k.psum("mlpB", [128, 1024], stk=stk)
        pT = k.psum("mlpT", [128, 512], stk=stk)
        wqv = wqkT.rearrange("(dt p) o -> p dt o", p=128)
        wvv = wvT.rearrange("(dt p) o -> p dt o", p=128)
        wov = wogT.rearrange("(dt p) o -> p dt o", p=128)
        k.barrier()

        k.load("sync", [(ifbt[:], ifbias)])

        def s15i(v):
            return v.memset(s15[:], 1.0 / 15.0)
        k.step("vector", s15i)
        k.load("gpsimd", [(wifb[:].rearrange("p (dt o) -> p dt o", dt=ND),
                           wifT.rearrange("(dt p) o -> p dt o", p=128))])
        m_mm = {}
        m_evc = {}
        st_q = {}

        for g in range(NG):
            gsl = slice(g * GT, (g + 1) * GT)
            k.load("sync", [(xg[:].rearrange("p (dt t) -> p dt t", dt=ND),
                             xin[:, gsl].rearrange("(dt p) t -> p dt t", p=128))])
            mark_apply = emit_ln(k, ln, xg, xlng, pA, pB)

            for ot in range(16):
                buf = ot % 2
                osl = slice(ot * 128, (ot + 1) * 128)
                wl = k.load("gpsimd",
                            [(wblk[:, buf * 2048:buf * 2048 + 2048].rearrange(
                                "p (dt h) -> p dt h", dt=ND), wqv[:, :, osl])],
                            waits=[m_mm.get(("qk", g, ot - 2),
                                            m_mm.get(("w", 1, g - 1, 2, 3), ZMARK))])
                bank = pA[:, 0:512] if buf == 0 else pA[:, 512:1024]

                def mmqk(t, buf=buf, bank=bank):
                    inst = None
                    for dt in range(ND):
                        inst = t.matmul(bank, wblk[:, buf * 2048 + dt * 128:buf * 2048 + (dt + 1) * 128],
                                        xlng[:, dt * GT:(dt + 1) * GT],
                                        start=(dt == 0), stop=(dt == ND - 1))
                    return inst
                m_mm[("qk", g, ot)] = k.step(
                    "tensor", mmqk,
                    waits=[(k.gld, wl), mark_apply,
                           m_evc.get(("qk", g, ot - 2), m_evc.get(("last", g - 1), ZMARK))])

                def ev(v, buf=buf, bank=bank):
                    return v.tensor_copy(qko[:, buf * 512:(buf + 1) * 512], bank)
                m_evc[("qk", g, ot)] = k.step(
                    "vector", ev,
                    waits=[m_mm[("qk", g, ot)],
                           (k.sld, st_q.get(("qk", g, ot - 2),
                                            st_q.get(("w", 1, g - 1, 3, 2 + (ot % 2)), 0)))])
                st_q[("qk", g, ot)] = k.load(
                    "sync", [(contrib[ot * 128:(ot + 1) * 128, gsl],
                              qko[:, buf * 512:(buf + 1) * 512])],
                    waits=[m_evc[("qk", g, ot)]])

            prev_ev = m_evc[("qk", g, 15)]
            for which, wsrc in ((0, wvv), (1, wov)):
                for vb in range(4):
                    buf = vb % 2
                    vsl = slice(vb * 512, (vb + 1) * 512)
                    if which == 0 and vb == 0:
                        wfall = m_mm[("qk", g, 15)]
                    elif which == 0 and vb == 1:
                        wfall = m_mm.get(("w", 1, g - 1, 3, 3), ZMARK)
                    elif which == 1 and vb < 2:
                        wfall = m_mm[("w", 0, g, 2 + vb, 3)]
                    else:
                        wfall = ZMARK
                    wl = k.load("gpsimd",
                                [(wblk[:, buf * 8192:(buf + 1) * 8192].rearrange(
                                    "p (dt h) -> p dt h", dt=ND), wsrc[:, :, vsl])],
                                waits=[m_mm.get(("w", which, g, vb - 2, 3), wfall)])
                    for ttl in range(4):
                        pbuf = ttl % 2
                        bank = pB[:, 0:512] if pbuf == 0 else pB[:, 512:1024]

                        def mmv(t, ttl=ttl, buf=buf, bank=bank):
                            inst = None
                            for dt in range(ND):
                                inst = t.matmul(bank,
                                                xlng[:, dt * GT + ttl * 128:dt * GT + (ttl + 1) * 128],
                                                wblk[:, buf * 8192 + dt * 512:buf * 8192 + (dt + 1) * 512],
                                                start=(dt == 0), stop=(dt == ND - 1))
                            return inst
                        m_mm[("w", which, g, vb, ttl)] = k.step(
                            "tensor", mmv,
                            waits=[(k.gld, wl), mark_apply,
                                   m_evc.get(("w", which, g, vb, ttl - 2),
                                             m_evc.get(("w", which, g, vb - 1, 2 + ttl),
                                                       prev_ev))])
                        qbuf = (vb * 4 + ttl) % 2

                        if which == 0:
                            def ev2(v, qbuf=qbuf, bank=bank):
                                return v.tensor_copy(qko[:, qbuf * 512:(qbuf + 1) * 512], bank)
                            m_evc[("w", which, g, vb, ttl)] = k.step(
                                "vector", ev2,
                                waits=[m_mm[("w", which, g, vb, ttl)],
                                       (k.sld, st_q.get(("w", which, g, vb, ttl - 2),
                                                        st_q.get(("w", which, g, vb - 1, 2 + ttl),
                                                                 st_q.get(("qk", g, 14 + qbuf), 0))))])
                            dst = contrib[2048 + g * GT + ttl * 128:
                                          2048 + g * GT + (ttl + 1) * 128,
                                          vb * 512:(vb + 1) * 512]
                        else:
                            def ev2(s, qbuf=qbuf, bank=bank):
                                return s.activation(qko[:, qbuf * 512:(qbuf + 1) * 512],
                                                    bank, AF.Sigmoid)
                            m_evc[("w", which, g, vb, ttl)] = k.step(
                                "scalar", ev2,
                                waits=[m_mm[("w", which, g, vb, ttl)],
                                       (k.sld, st_q.get(("w", which, g, vb, ttl - 2),
                                                        st_q.get(("w", which, g, vb - 1, 2 + ttl),
                                                                 st_q.get(("w", 0, g, 3, 2 + ttl), 0))))])
                            dst = og_l[g * GT + ttl * 128:g * GT + (ttl + 1) * 128,
                                       vb * 512:(vb + 1) * 512]
                        st_q[("w", which, g, vb, ttl)] = k.load(
                            "sync", [(dst, qko[:, qbuf * 512:(qbuf + 1) * 512])],
                            waits=[m_evc[("w", which, g, vb, ttl)]])
                prev_ev = m_evc[("w", which, g, 3, 3)]

            for ttl in range(4):
                def mmif(t, ttl=ttl):
                    inst = None
                    for dt in range(ND):
                        inst = t.matmul(pT[:, 0:16],
                                        xlng[:, dt * GT + ttl * 128:dt * GT + (ttl + 1) * 128],
                                        wifb[:, dt * 16:(dt + 1) * 16],
                                        start=(dt == 0), stop=(dt == ND - 1))
                    return inst
                k.step("tensor", mmif)

                def addb(v):
                    return v.tensor_add(sc1[:], pT[:, 0:16], ifbt[:])
                k.step("vector", addb)

                def cap1(s):
                    return s.activation(sc1[:], sc1[:], AF.Tanh, scale=s15[:])
                k.step("scalar", cap1)

                def cap2(v):
                    return v.tensor_scalar_mul(sc1[:], sc1[:], 15.0)
                k.step("vector", cap2)

                def trif(t):
                    return t.transpose(pT[0:16, 256:384], sc1[:], i128f)
                k.step("tensor", trif)

                def evif(v):
                    return v.tensor_copy(ifr[:], pT[0:16, 256:384])
                k.step("vector", evif)
                k.load("sync", [(contrib_if[:, g * GT + ttl * 128:g * GT + (ttl + 1) * 128],
                                 ifr[:])])
            m_evc[("last", g)] = m_evc[("w", 1, g, 3, 3)]

    k.collective("AllGather", ALU.bypass, [contrib], [ag])
    k.collective("AllGather", ALU.bypass, [contrib_if], [agif])

    qcm = k.dram("ml_qc", [1024, 2048], BF16)
    kcm = k.dram("ml_kc", [1024, 2048], BF16)
    vcm = k.dram("ml_vc", [TOK, 256], BF16)
    ifcm = k.dram("ml_ifc", [4, S], F32)

    def mcanon(sy):
        pid = nc.partition_id([sy.engine])
        agv = ag.rearrange("(b rl half sec hh p) t -> b rl half sec hh p t",
                           b=2, rl=4, half=2, sec=2, hh=8)
        qdst = qcm.rearrange("(b rl p) t -> b rl p t", b=2, rl=4)
        kdst = kcm.rearrange("(b rl p) t -> b rl p t", b=2, rl=4)
        sy.dma_start(out=qdst, in_=agv[:, :, 0, 0, bass.ds(pid, 1)].squeeze(axis=2)
                     ).then_inc(k.sld.h, 16)
        k.sld.n += 16
        sy.dma_start(out=kdst, in_=agv[:, :, 0, 1, bass.ds(pid, 1)].squeeze(axis=2)
                     ).then_inc(k.sld.h, 16)
        k.sld.n += 16
        agw = ag.rearrange("(b rl half tok) (hv cv) -> b rl half tok hv cv",
                           b=2, rl=4, half=2, hv=8)
        vdst = vcm.rearrange("(b rl tok) c -> b rl tok c", b=2, rl=4)
        for bb in range(2):
            for rl in range(4):
                sy.dma_start(out=vdst[bb, rl],
                             in_=agw[bb, rl, 1, :, bass.ds(pid, 1), :].squeeze(axis=1)
                             ).then_inc(k.sld.h, 16)
                k.sld.n += 16
        agi = agif.rearrange("(b rl w x) t -> b rl w x t", b=2, rl=4, w=2)
        ifdst = ifcm.rearrange("(b x) (rl t) -> b rl x t", b=2, rl=4)
        for x in range(2):
            sy.dma_start(out=ifdst[:, :, x],
                         in_=agi[:, :, x, bass.ds(pid, 1)].squeeze(axis=2)
                         ).then_inc(k.sld.h, 16)
            k.sld.n += 16
    k.step("sync", mcanon)

    NCH = S // 128
    with ExitStack() as stk:
        Qf = k.sb("mlQ", [128, S], BF16, stk=stk)
        Kf = k.sb("mlK", [128, S], BF16, stk=stk)
        vt = k.sb("mlV", [128, 64 * 257], BF16, stk=stk)
        irow = k.sb("mli", [1, S], F32, stk=stk)
        frow = k.sb("mlf", [1, S], F32, stk=stk)
        ones1 = k.sb("mlo1", [1, 128], F32, stk=stk)
        neg1 = k.sb("mln1", [1, 1], F32, stk=stk)
        one1 = k.sb("mlp1", [1, 1], F32, stk=stk)
        negc = k.sb("mlnc", [128, 1], F32, stk=stk)
        erow = k.sb("mler", [1, 128], F32, stk=stk)
        lrow = k.sb("mllr", [1, 128], F32, stk=stk)
        bneg = k.sb("mlbn", [1, 128], F32, stk=stk)
        grow = k.sb("mlgr", [1, 128], F32, stk=stk)
        mrow = k.sb("mlmr", [1, 128], F32, stk=stk)
        mrow2 = k.sb("mlm2", [1, 128], F32, stk=stk)
        mmrow = k.sb("mlmm", [1, 128], F32, stk=stk)
        rowsb = k.sb("mlrb", [1, 256], BF16, stk=stk)
        gsbl = k.sb("mlgl", [128, 1], F32, stk=stk)
        gsbr = k.sb("mlgw", [128, 1], F32, stk=stk)
        msb = k.sb("mlms", [128, 1], F32, stk=stk)
        m2sb = k.sb("mlm2s", [128, 1], F32, stk=stk)
        emc = k.sb("mlem", [128, 1], F32, stk=stk)
        aq = k.sb("mlaq", [128, 1], F32, stk=stk)
        den = k.sb("mlden", [128, 1], F32, stk=stk)
        rden = k.sb("mlrd", [128, 1], F32, stk=stk)
        mcol = k.sb("mlmc", [128, 1], F32, stk=stk)
        mlns = k.sb("mlml", [128, 1], F32, stk=stk)
        ccol = k.sb("mlcc", [128, 1], F32, stk=stk)
        phc = k.sb("mlph", [128, 1], F32, stk=stk)
        wc = k.sb("mlwc", [128, 1], F32, stk=stk)
        bnl = k.sb("mlbl", [128, 1], F32, stk=stk)
        Dt = k.sb("mlDt", [128, 128], F32, stk=stk)
        ptb = k.sb("mlpt", [128, 128], BF16, stk=stk)
        kw = k.sb("mlkw", [128, 128], BF16, stk=stk)
        CnA = k.sb("mlCa", [128, 257], F32, stk=stk)
        CnB = k.sb("mlCb", [128, 257], F32, stk=stk)
        Cnb = k.sb("mlCc", [128, 257], BF16, stk=stk)
        hIs = k.sb("mlhi", [128, 257], F32, stk=stk)
        hps = k.sb("mlhp", [128, 257], F32, stk=stk)
        hsc = k.sb("mlhs", [128, 256], F32, stk=stk)
        hsq = k.sb("mlhq", [128, 256], F32, stk=stk)
        srn = k.sb("mlsr", [128, 4], F32, stk=stk)
        hnn = k.sb("mlhn", [128, 256], BF16, stk=stk)
        gmt = k.sb("mlgm", [128, 256], F32, stk=stk)
        pMB = k.psum("mlMB", [128, 256], stk=stk)
        pS = k.psum("mlpS", [128, 128], stk=stk)
        pI = k.psum("mlpI", [128, 512], stk=stk)
        pPT = k.psum("mlpP", [128, 512], stk=stk)
        pD = k.psum("mlpD", [128, 512], stk=stk)
        pG1 = k.psum("mlg1", [128, 2], stk=stk)
        pG2 = k.psum("mlg2", [128, 2], stk=stk)
        pKt = k.psum("mlKt", [128, 128], BF16, stk=stk)
        vtv = vt[:].rearrange("p (c w) -> p c w", w=257)

        def initc(v):
            v.memset(ones1[:], 1.0)
            v.memset(neg1[:], -1.0)
            v.memset(one1[:], 1.0)
            v.memset(negc[:], -1.0)
            return v.memset(vtv[:, :, 256:257], 1.0)
        k.step("vector", initc)
        k.load("sync", [(gmt[:], gmt_in)])

        for slot in range(2):
            qk = [(Qf[:, rl * 2048:(rl + 1) * 2048],
                   qcm[slot * 512 + rl * 128:slot * 512 + (rl + 1) * 128, :])
                  for rl in range(4)]
            qk += [(Kf[:, rl * 2048:(rl + 1) * 2048],
                    kcm[slot * 512 + rl * 128:slot * 512 + (rl + 1) * 128, :])
                   for rl in range(4)]
            qk.append((vtv[:, :, 0:256],
                       vcm[slot * 8192:(slot + 1) * 8192, :].rearrange("(c p) w -> p c w", p=128)))
            qk.append((irow[:], ifcm[2 * slot:2 * slot + 1, :]))
            qk.append((frow[:], ifcm[2 * slot + 1:2 * slot + 2, :]))
            k.load("sync", qk)

            def zero_state(v):
                v.memset(CnA[:], 0.0)
                v.memset(Cnb[:], 0.0)
                return v.memset(mcol[:], 0.0)
            k.step("vector", zero_state)

            Cn_cur, Cn_nxt = CnA, CnB
            for c in range(NCH):
                csl = slice(c * 128, (c + 1) * 128)

                def e1(s, csl=csl):
                    return s.activation(erow[:], frow[0:1, csl], AF.Exp, scale=neg1[:])
                k.step("scalar", e1)

                def l1(s):
                    return s.activation(lrow[:], erow[:], AF.Ln, bias=one1[:])
                k.step("scalar", l1)

                def cums(v):
                    return v.tensor_tensor_scan(bneg[:], ones1[:], lrow[:], 0.0,
                                                ALU.mult, ALU.add)
                k.step("vector", cums)

                def gr(v, csl=csl):
                    return v.tensor_add(grow[:], irow[0:1, csl], bneg[:])
                k.step("vector", gr)

                def cumm(v):
                    return v.tensor_tensor_scan(mrow[:], ones1[:], grow[:], -1e30,
                                                ALU.mult, ALU.max)
                k.step("vector", cumm)

                def mmax(v):
                    return v.tensor_scalar_max(mrow2[:], mrow[:], mcol[0:1, 0:1])
                k.step("vector", mmax)

                def mmr(v):
                    return v.tensor_sub(mmrow[:], mrow2[:], bneg[:])
                k.step("vector", mmr)

                def rcast(v):
                    v.tensor_copy(rowsb[0:1, 0:128], mrow2[:])
                    return v.tensor_copy(rowsb[0:1, 128:256], bneg[:])
                k.step("vector", rcast)

                def bc(t):
                    t.matmul(pMB[:, 0:128], ln.onesrow[:], rowsb[0:1, 0:128],
                             start=True, stop=False)
                    t.matmul(pMB[:, 0:128], i128bf, trimp,
                             start=False, stop=False, skip_group_check=True)
                    return t.matmul(pMB[:, 128:256], ln.onesrow[:], rowsb[0:1, 128:256],
                                    start=False, stop=True, skip_group_check=True)
                k.step("tensor", bc)

                def trg(t):
                    return t.transpose(pG1[:, 0:1], grow[:], i128f[0:1, 0:1])
                k.step("tensor", trg)

                def trm(t):
                    return t.transpose(pG2[:, 0:1], mmrow[:], i128f[0:1, 0:1])
                k.step("tensor", trm)

                def gl(v):
                    return v.tensor_scalar_add(gsbl[:], pG1[:, 0:1], LNS128)
                k.step("vector", gl)

                def gw(v):
                    return v.tensor_copy(gsbr[:], pG1[:, 0:1])
                k.step("vector", gw)

                def ms(v):
                    return v.tensor_copy(msb[:], pG2[:, 0:1])
                k.step("vector", ms)

                def trm2(t):
                    return t.transpose(pG1[:, 0:1], mrow2[:], i128f[0:1, 0:1])
                k.step("tensor", trm2)

                def m2c(v):
                    return v.tensor_copy(m2sb[:], pG1[:, 0:1])
                k.step("vector", m2c)

                def smm(t, csl=csl):
                    return t.matmul(pS[:], Kf[:, csl], Qf[:, csl], start=True, stop=True)
                k.step("tensor", smm)

                def dexp(s):
                    return s.activation(Dt[:], pMB[:, 0:128], AF.Exp,
                                        scale=negc[:], bias=gsbl[:])
                k.step("scalar", dexp)

                def pmul(v):
                    return v.tensor_mul(ptb[:], pS[:], Dt[:])
                k.step("vector", pmul)

                def immm(t, csl=csl):
                    return t.matmul(pI[:, 0:257], Qf[:, csl], Cnb[:], start=True, stop=True)
                k.step("tensor", immm)

                def tmm(t, c=c):
                    return t.matmul(pPT[:, 0:257], ptb[:], vtv[:, c, :], start=True, stop=True)
                k.step("tensor", tmm)

                def mln(v):
                    return v.tensor_scalar_add(mlns[:], mcol[:], LNS128)
                k.step("vector", mln)

                def cc2(s):
                    return s.activation(ccol[:], m2sb[:], AF.Exp,
                                        scale=negc[:], bias=mlns[:])
                k.step("scalar", cc2)

                def hi(s):
                    return s.activation(hIs[:], pI[:, 0:257], AF.Copy, scale=ccol[:])
                k.step("scalar", hi)

                def hp(v):
                    return v.tensor_add(hps[:], hIs[:], pPT[:, 0:257])
                k.step("vector", hp)

                def em(s):
                    return s.activation(emc[:], msb[:], AF.Exp, scale=negc[:])
                k.step("scalar", em)

                def aqs(s):
                    return s.activation(aq[:], hps[:, 256:257], AF.Abs)
                k.step("scalar", aqs)

                def dn(v):
                    return v.tensor_max(den[:], aq[:], emc[:])
                k.step("vector", dn)

                def dn2(v):
                    return v.tensor_scalar_add(den[:], den[:], EPS_ML)
                k.step("vector", dn2)

                def rd(v):
                    return v.reciprocal(rden[:], den[:])
                k.step("vector", rd)

                def hh(s):
                    return s.activation(hsc[:], hps[:, 0:256], AF.Copy, scale=rden[:])
                k.step("scalar", hh)

                def n1(v):
                    return v.reduce_sum(srn[:, 0:1], hsc[:], mybir.AxisListType.X)
                k.step("vector", n1)

                def n2(s):
                    return s.activation(hsq[:], hsc[:], AF.Square)
                k.step("scalar", n2)

                def n3(v):
                    return v.reduce_sum(srn[:, 1:2], hsq[:], mybir.AxisListType.X)
                k.step("vector", n3)

                def n4(v):
                    v.tensor_scalar_mul(srn[:, 0:1], srn[:, 0:1], 1.0 / 256.0)
                    return v.tensor_scalar_mul(srn[:, 1:2], srn[:, 1:2], 1.0 / 256.0)
                k.step("vector", n4)

                def n5(v):
                    return v.tensor_mul(srn[:, 2:3], srn[:, 0:1], srn[:, 0:1])
                k.step("vector", n5)

                def n6(v):
                    return v.tensor_sub(srn[:, 2:3], srn[:, 1:2], srn[:, 2:3])
                k.step("vector", n6)

                def n7(v):
                    return v.tensor_scalar_add(srn[:, 2:3], srn[:, 2:3], EPS_ML)
                k.step("vector", n7)

                def n8(s):
                    return s.activation(srn[:, 3:4], srn[:, 2:3], AF.Sqrt)
                k.step("scalar", n8)

                def n9(v):
                    return v.reciprocal(srn[:, 2:3], srn[:, 3:4])
                k.step("vector", n9)

                def n10(v):
                    return v.tensor_scalar(hsq[:], hsc[:], srn[:, 0:1], srn[:, 2:3],
                                           ALU.subtract, ALU.mult)
                k.step("vector", n10)

                def n11(v):
                    return v.tensor_mul(hnn[:], hsq[:], gmt[:])
                k.step("vector", n11)
                k.load("sync", [(contrib_h[slot * S + c * 128:slot * S + (c + 1) * 128, :],
                                 hnn[:])])

                def wcs(s):
                    return s.activation(wc[:], pMB[:, 127:128], AF.Exp,
                                        scale=negc[:], bias=gsbr[:])
                k.step("scalar", wcs)

                def ktr(t, csl=csl):
                    return t.transpose(pKt[:], Kf[:, csl], i128bf)
                k.step("tensor", ktr)

                def kws(v):
                    return v.tensor_scalar_mul(kw[:], pKt[:], wc[:])
                k.step("vector", kws)

                def dmm(t, c=c):
                    return t.matmul(pD[:, 0:257], kw[:], vtv[:, c, :], start=True, stop=True)
                k.step("tensor", dmm)

                def phs(s):
                    return s.activation(phc[:], pMB[:, 127:128], AF.Exp,
                                        scale=negc[:], bias=mcol[:])
                k.step("scalar", phs)

                def cnew(v, Cn_cur=Cn_cur, Cn_nxt=Cn_nxt):
                    return v.scalar_tensor_tensor(Cn_nxt[:], Cn_cur[:], phc[:],
                                                  pD[:, 0:257], ALU.mult, ALU.add)
                k.step("vector", cnew)

                def ccast(v, Cn_nxt=Cn_nxt):
                    return v.tensor_copy(Cnb[:], Cn_nxt[:])
                k.step("vector", ccast)

                def bn2(v):
                    return v.tensor_copy(bnl[:], pMB[:, 255:256])
                k.step("vector", bn2)

                def mup(v):
                    return v.tensor_sub(mcol[:], pMB[:, 127:128], bnl[:])
                k.step("vector", mup)

                Cn_cur, Cn_nxt = Cn_nxt, Cn_cur

    if ML_DBG:
        dbg_ifc = k.dram("dbg_ifc", [4, S], F32, kind="ExternalOutput")
        dbg_ch = k.dram("dbg_ch", [2 * S, 256], BF16, kind="ExternalOutput")
        dbg_qcm = k.dram("dbg_qcm", [1024, 2048], BF16, kind="ExternalOutput")
        dbg_kcm = k.dram("dbg_kcm", [1024, 2048], BF16, kind="ExternalOutput")
        dbg_vcm = k.dram("dbg_vcm", [TOK, 256], BF16, kind="ExternalOutput")
        k.load("sync", [(dbg_ifc, ifcm), (dbg_ch, contrib_h), (dbg_qcm, qcm),
                        (dbg_kcm, kcm), (dbg_vcm, vcm)])
    k.collective("AllGather", ALU.bypass, [contrib_h], [agh])

    hcc = k.dram("ml_hcc", [8 * TPC, 256], BF16)

    def hcanon(sy):
        pid = nc.partition_id([sy.engine])
        agv = agh.rearrange("(r b q) w -> r b (q w)", r=8, b=2)
        hdst = hcc.rearrange("(r q) w -> r (q w)", r=8)
        src = agv[:, bass.ds(pid // 4, 1), bass.ds((pid % 4) * (2048 * 256), 2048 * 256)]
        sy.dma_start(out=hdst, in_=src.squeeze(axis=1)).then_inc(k.sld.h, 16)
        k.sld.n += 16
    k.step("sync", hcanon)

    emit_outproj(k, hcc, 256, xin, xout, woutt, i128bf, gate_dram=og_l)


def build_mlstm():
    k = KB()
    xin = k.dram("xin", [D, TPC], F32, kind="ExternalInput")
    wqkT = k.dram("wqkT", [D, 2048], F32, kind="ExternalInput")
    wvT = k.dram("wvT", [D, D], F32, kind="ExternalInput")
    wogT = k.dram("wogT", [D, D], F32, kind="ExternalInput")
    wifT = k.dram("wifT", [D, 16], F32, kind="ExternalInput")
    ifbias = k.dram("ifbias", [128, 16], F32, kind="ExternalInput")
    gmt_in = k.dram("gmt", [128, 256], F32, kind="ExternalInput")
    woutt = k.dram("woutt", [D, D], F32, kind="ExternalInput")
    i128 = k.dram("i128", [128, 128], BF16, kind="ExternalInput")
    i128f_d = k.dram("i128f", [128, 128], F32, kind="ExternalInput")
    trimp_d = k.dram("trimp", [128, 128], BF16, kind="ExternalInput")
    xout = k.dram("xout", [D, TPC], F32, kind="ExternalOutput")
    ln = LnEnv(k)
    i128sb = k.sb("i128s", [128, 128], BF16)
    i128fsb = k.sb("i128fs", [128, 128], F32)
    trisb = k.sb("trps", [128, 128], BF16)
    k.load("sync", [(i128sb[:], i128), (i128fsb[:], i128f_d), (trisb[:], trimp_d)])
    emit_mlstm(k, ln, xin, xout, wqkT, wvT, wogT, wifT, ifbias, gmt_in,
               woutt, i128sb[:], i128fsb[:], trisb[:])
    k.finish()
    return k


def build_attn():
    k = KB()
    xin = k.dram("xin", [D, TPC], F32, kind="ExternalInput")
    wqkvt = k.dram("wqkvt", [D, 3 * D], F32, kind="ExternalInput")
    woutt = k.dram("woutt", [D, D], F32, kind="ExternalInput")
    i128 = k.dram("i128", [128, 128], BF16, kind="ExternalInput")
    trim = k.dram("trim", [128, 128], BF16, kind="ExternalInput")
    xout = k.dram("xout", [D, TPC], F32, kind="ExternalOutput")
    contrib = k.dram("at_contrib", [3 * D, TPC], BF16)
    ag = k.dram("at_ag", [NCORE * 3 * D, TPC], BF16, addr_space="Shared")
    contrib_o = k.dram("at_co", [4 * S, 128], BF16)
    ago = k.dram("at_ago", [NCORE * 4 * S, 128], BF16, addr_space="Shared")
    ln = LnEnv(k)
    i128sb = k.sb("i128s", [128, 128], BF16)
    trisb = k.sb("trims", [128, 128], BF16)
    k.load("sync", [(i128sb[:], i128), (trisb[:], trim)])
    emit_attn(k, ln, xin, xout, wqkvt, woutt, i128sb[:], trisb[:],
              contrib, ag, contrib_o, ago)
    k.finish()
    return k


def build_ffn1():
    k = KB()
    xin = k.dram("xin", [D, TPC], F32, kind="ExternalInput")
    wgt = k.dram("wgt", [D, H1], I32, kind="ExternalInput")
    wut = k.dram("wut", [D, H1], I32, kind="ExternalInput")
    wdt = k.dram("wdt", [H1, D], I32, kind="ExternalInput")
    sg = k.dram("sg", [128, NH1T], F32, kind="ExternalInput")
    su = k.dram("su", [128, NH1T], F32, kind="ExternalInput")
    sd = k.dram("sd", [128, ND], F32, kind="ExternalInput")
    xout = k.dram("xout", [D, TPC], F32, kind="ExternalOutput")
    ln = LnEnv(k)
    emit_ffn1(k, ln, xin, xout, wgt, wut, wdt, sg, su, sd)
    k.finish()
    return k



def _t(a):
    return np.ascontiguousarray(np.asarray(a).T)


def _hadamard(n):
    h = np.array([[1.0]], dtype=np.float32)
    while h.shape[0] < n:
        h = np.block([[h, h], [h, -h]])
    return h


def prep_weights(inp):
    w = {}
    w["w1t"] = _t(inp["L0_ffn_W1"])
    w["w3t"] = _t(inp["L0_ffn_W3"])
    w["w2t"] = _t(inp["L0_ffn_W2"])
    w["h128"] = _hadamard(128).astype(BF)
    w["wgt"] = _t(inp["L1_ffn_wq_gate"])
    w["wut"] = _t(inp["L1_ffn_wq_up"])
    w["wdt"] = _t(inp["L1_ffn_wq_down"])
    w["sg"] = _t(np.asarray(inp["L1_ffn_s_gate"], np.float32).reshape(NH1T, 128))
    w["su"] = _t(np.asarray(inp["L1_ffn_s_up"], np.float32).reshape(NH1T, 128))
    w["sd"] = _t(np.asarray(inp["L1_ffn_s_down"], np.float32).reshape(ND, 128))
    w["wqkvt"] = _t(inp["L1_attn_W_qkv"])
    w["woutt1"] = _t(inp["L1_attn_W_out"])
    w["wqkT"] = np.ascontiguousarray(
        np.concatenate([np.asarray(inp["L0_attn_Wq"]),
                        np.asarray(inp["L0_attn_Wk"])], 0).T)
    w["wvT"] = _t(inp["L0_attn_Wv"])
    w["wogT"] = _t(inp["L0_attn_Wog"])
    w["wifT"] = np.ascontiguousarray(
        np.concatenate([np.asarray(inp["L0_attn_Wig"]),
                        np.asarray(inp["L0_attn_Wfg"])], 0).T)
    w["ifbias"] = np.tile(np.concatenate(
        [np.asarray(inp["L0_attn_big"]),
         np.asarray(inp["L0_attn_bfg"])]).astype(np.float32)[None, :], (128, 1))
    w["woutt0"] = _t(inp["L0_attn_Wout"])
    w["gamma"] = np.asarray(inp["L0_attn_gamma"], np.float32)
    w["i128"] = np.eye(128, dtype=BF)
    w["i128f"] = np.eye(128, dtype=np.float32)
    w["trim"] = np.where(np.arange(128)[:, None] > np.arange(128)[None, :],
                         -30000.0, 0.0).astype(BF)
    w["trimp"] = np.where(np.arange(128)[:, None] > np.arange(128)[None, :],
                          30000.0, 0.0).astype(BF)
    return w


def run_launch(nc, in_maps):
    res = run_bass_kernel_spmd(nc, in_maps, core_ids=list(range(NCORE)))
    return res.results



def emit_precast(k: KB, pairs):
    with ExitStack() as stk:
        buf = k.sb("pcb", [128, 2 * 8192], BF16, stk=stk)
        m_st = {}
        i = 0
        for dst, srcd, R, C in pairs:
            for r0 in range(0, R, 128):
                par = i % 2
                wl = k.load("gpsimd",
                            [(buf[:, par * 8192:par * 8192 + C],
                              srcd[r0:r0 + 128, :])],
                            waits=[(k.sld, m_st.get(i - 2, 0))])
                m_st[i] = k.load("sync",
                                 [(dst[r0:r0 + 128, :],
                                   buf[:, par * 8192:par * 8192 + C])],
                                 waits=[(k.gld, wl)])
                i += 1


def build_w2prep():
    k = KB()
    w2t = k.dram("w2t", [H0, D], F32, kind="ExternalInput")
    h128 = k.dram("h128", [128, 128], BF16, kind="ExternalInput")
    out = k.dram("w2ht_part", [H0, DW], BF16, kind="ExternalOutput")
    emit_w2prep(k, w2t, h128, out)
    k.finish()
    return k


def build_ffn0():
    k = KB()
    xin = k.dram("xin", [D, TPC], F32, kind="ExternalInput")
    w1t = k.dram("w1t", [D, H0], F32, kind="ExternalInput")
    w3t = k.dram("w3t", [D, H0], F32, kind="ExternalInput")
    w2ht = k.dram("w2ht", [H0, D], BF16, kind="ExternalInput")
    xout = k.dram("xout", [D, TPC], F32, kind="ExternalOutput")
    ln = LnEnv(k)

    def w2pairs(w2view, sb, rnd):
        return [(w2view.rearrange("p (hh o) -> p hh o", hh=8),
                 w2ht[sb * 1024:(sb + 1) * 1024,
                      rnd * 1024:rnd * 1024 + 1024].rearrange("(hh p) o -> p hh o", p=128))]
    emit_ffn0(k, ln, xin, xout, w1t, w3t, w2pairs)
    k.finish()
    return k


def build_fused():
    k = KB()
    xin = k.dram("xin", [D, TPC], F32, kind="ExternalInput")
    w2t = k.dram("w2t", [H0, D], F32, kind="ExternalInput")
    h128 = k.dram("h128", [128, 128], BF16, kind="ExternalInput")
    w1t = k.dram("w1t", [D, H0], F32, kind="ExternalInput")
    w3t = k.dram("w3t", [D, H0], F32, kind="ExternalInput")
    wqkT = k.dram("wqkT", [D, 2048], F32, kind="ExternalInput")
    wvT = k.dram("wvT", [D, D], F32, kind="ExternalInput")
    wogT = k.dram("wogT", [D, D], F32, kind="ExternalInput")
    wifT = k.dram("wifT", [D, 16], F32, kind="ExternalInput")
    ifbias = k.dram("ifbias", [128, 16], F32, kind="ExternalInput")
    gmt_in = k.dram("gmt", [128, 256], F32, kind="ExternalInput")
    woutt0 = k.dram("woutt0", [D, D], F32, kind="ExternalInput")
    wgt = k.dram("wgt", [D, H1], I32, kind="ExternalInput")
    wut = k.dram("wut", [D, H1], I32, kind="ExternalInput")
    wdt = k.dram("wdt", [H1, D], I32, kind="ExternalInput")
    sg = k.dram("sg", [128, NH1T], F32, kind="ExternalInput")
    su = k.dram("su", [128, NH1T], F32, kind="ExternalInput")
    sd = k.dram("sd", [128, ND], F32, kind="ExternalInput")
    wqkvt = k.dram("wqkvt", [D, 3 * D], F32, kind="ExternalInput")
    woutt1 = k.dram("woutt1", [D, D], F32, kind="ExternalInput")
    i128 = k.dram("i128", [128, 128], BF16, kind="ExternalInput")
    i128f_d = k.dram("i128f", [128, 128], F32, kind="ExternalInput")
    trim_d = k.dram("trim", [128, 128], BF16, kind="ExternalInput")
    trimp_d = k.dram("trimp", [128, 128], BF16, kind="ExternalInput")
    xout = k.dram("xout", [D, TPC], F32, kind="ExternalOutput")

    xa = k.dram("x_a", [D, TPC], F32)
    xb = k.dram("x_b", [D, TPC], F32)
    w2part = k.dram("w2part", [H0, DW], BF16)
    agw2 = k.dram("agw2", [NCORE * H0, DW], BF16, addr_space="Shared")

    w1b16 = k.dram("w1b16", [D, H0], BF16)
    w3b16 = k.dram("w3b16", [D, H0], BF16)
    wgb16 = k.dram("wgb16", [D, H1], BF16)
    wub16 = k.dram("wub16", [D, H1], BF16)
    wdb16 = k.dram("wdb16", [H1, D], BF16)
    emit_precast(k, [(w1b16, w1t, D, H0), (w3b16, w3t, D, H0),
                     (wgb16, wgt, D, H1), (wub16, wut, D, H1),
                     (wdb16, wdt, H1, D)])
    emit_w2prep(k, w2t, h128, w2part)
    k.collective("AllGather", ALU.bypass, [w2part], [agw2])

    ln = LnEnv(k)
    i128sb = k.sb("i128s", [128, 128], BF16)
    i128fsb = k.sb("i128fs", [128, 128], F32)
    trisb = k.sb("tris", [128, 128], BF16)
    tripsb = k.sb("trps", [128, 128], BF16)
    k.load("sync", [(i128sb[:], i128), (i128fsb[:], i128f_d),
                    (trisb[:], trim_d), (tripsb[:], trimp_d)])

    def w2pairs(w2view, sb, rnd):
        wv = w2view.rearrange("p (hh qq o) -> p hh qq o", hh=8, qq=4)
        return [(wv[:, :, q, :],
                 agw2[(rnd * 4 + q) * H0 + sb * 1024:(rnd * 4 + q) * H0 + (sb + 1) * 1024,
                      :].rearrange("(hh p) o -> p hh o", p=128))
                for q in range(4)]

    emit_ffn0(k, ln, xin, xa, w1b16, w3b16, w2pairs, cast=False)
    emit_mlstm(k, ln, xa, xb, wqkT, wvT, wogT, wifT, ifbias, gmt_in,
               woutt0, i128sb[:], i128fsb[:], tripsb[:])
    emit_ffn0(k, ln, xb, xa, w1b16, w3b16, w2pairs, cast=False)
    emit_ffn1(k, ln, xa, xb, wgb16, wub16, wdb16, sg, su, sd)
    emit_attn(k, ln, xb, xa, wqkvt, woutt1, i128sb[:], trisb[:],
              k.dram("at_contrib", [3 * D, TPC], BF16),
              k.dram("at_ag", [NCORE * 3 * D, TPC], BF16, addr_space="Shared"),
              k.dram("at_co", [4 * S, 128], BF16),
              k.dram("at_ago", [NCORE * 4 * S, 128], BF16, addr_space="Shared"))
    emit_ffn1(k, ln, xa, xout, wgb16, wub16, wdb16, sg, su, sd)
    k.finish()
    return k


FUSED = True


def kernel(**inputs):
    w = prep_weights(inputs)
    x0 = np.asarray(inputs["L0_attn_x"], dtype=np.float32).reshape(TOK, D)
    xs = [np.ascontiguousarray(x0[c * TPC:(c + 1) * TPC].T) for c in range(NCORE)]

    if FUSED:
        kf = build_fused()
        in_maps = []
        for c in range(NCORE):
            in_maps.append({
                "xin": xs[c], "w2t": w["w2t"], "h128": w["h128"],
                "w1t": w["w1t"], "w3t": w["w3t"],
                "wqkT": w["wqkT"], "wvT": w["wvT"], "wogT": w["wogT"],
                "wifT": w["wifT"], "ifbias": w["ifbias"],
                "gmt": np.tile(w["gamma"][c][None, :], (128, 1)),
                "woutt0": w["woutt0"],
                "wgt": w["wgt"], "wut": w["wut"], "wdt": w["wdt"],
                "sg": w["sg"], "su": w["su"], "sd": w["sd"],
                "wqkvt": w["wqkvt"], "woutt1": w["woutt1"],
                "i128": w["i128"], "i128f": w["i128f"],
                "trim": w["trim"], "trimp": w["trimp"],
            })
        res = run_launch(kf.nc, in_maps)
        out = np.concatenate([res[c]["xout"].T for c in range(NCORE)], axis=0)
        return np.ascontiguousarray(out.reshape(B, S, D).astype(np.float32))

    k0 = build_w2prep()
    res0 = run_launch(k0.nc, [{"w2t": w["w2t"], "h128": w["h128"]}] * NCORE)
    w2ht = np.concatenate([res0[c]["w2ht_part"] for c in range(NCORE)], axis=1)

    kf0 = build_ffn0()
    km = build_mlstm()
    kf1 = build_ffn1()
    ka = build_attn()

    def ffn0(xs):
        res = run_launch(kf0.nc, [{"xin": xs[c], "w1t": w["w1t"], "w3t": w["w3t"],
                                   "w2ht": w2ht} for c in range(NCORE)])
        return [res[c]["xout"] for c in range(NCORE)]

    def mlstm(xs):
        res = run_launch(km.nc, [{"xin": xs[c], "wqkT": w["wqkT"], "wvT": w["wvT"],
                                  "wogT": w["wogT"], "wifT": w["wifT"],
                                  "ifbias": w["ifbias"],
                                  "gmt": np.tile(w["gamma"][c][None, :], (128, 1)),
                                  "woutt": w["woutt0"], "i128": w["i128"],
                                  "i128f": w["i128f"], "trimp": w["trimp"]}
                                 for c in range(NCORE)])
        return [res[c]["xout"] for c in range(NCORE)]

    def ffn1(xs):
        res = run_launch(kf1.nc, [{"xin": xs[c], "wgt": w["wgt"], "wut": w["wut"],
                                   "wdt": w["wdt"], "sg": w["sg"], "su": w["su"],
                                   "sd": w["sd"]} for c in range(NCORE)])
        return [res[c]["xout"] for c in range(NCORE)]

    def attn(xs):
        res = run_launch(ka.nc, [{"xin": xs[c], "wqkvt": w["wqkvt"],
                                  "woutt": w["woutt1"], "i128": w["i128"],
                                  "trim": w["trim"]} for c in range(NCORE)])
        return [res[c]["xout"] for c in range(NCORE)]

    xs = ffn0(xs)
    xs = mlstm(xs)
    xs = ffn0(xs)
    xs = ffn1(xs)
    xs = attn(xs)
    xs = ffn1(xs)
    out = np.concatenate([xs[c].T for c in range(NCORE)], axis=0)
    return np.ascontiguousarray(out.reshape(B, S, D).astype(np.float32))


# version 08 (final)
# speedup vs baseline: 1.1727x; 1.1727x over previous
import numpy as np
import ml_dtypes
import concourse.bass as bass
import concourse.mybir as mybir
from concourse.bass_utils import run_bass_kernel_spmd
from contextlib import ExitStack

F32 = mybir.dt.float32
BF16 = mybir.dt.bfloat16
I32 = mybir.dt.int32
AF = mybir.ActivationFunctionType
ALU = mybir.AluOpType
BF = ml_dtypes.bfloat16

NCORE = 8
B, S, D = 2, 8192, 2048
TOK = B * S
TPC = TOK // NCORE
ND = D // 128
GT = 512
NG = TPC // GT
H0 = 8192
NH0T = H0 // 128
H1 = 5632
NH1T = H1 // 128
EPS_LN = 1e-5
W2SCALE = 0.5 / float(np.sqrt(H0))
DW = D // NCORE
ATTN_UPTO = 99
ATTN_DBG = False
ML_DBG = False
FORCE_SERIAL = False
SER_LN = False
SER_LN2 = False
SER_P1 = False
SER_P2 = False



class KB:

    def __init__(self):
        nc = bass.Bass("TRN2", target_bir_lowering=False, debug=False)
        self.nc = nc
        self.blk = nc.ctx.enter_context(nc.Block())
        self.evs = {n: self._sem("ev_" + n)
                    for n in ("vector", "scalar", "tensor", "gpsimd", "sync")}
        self.gld = self._sem("gld")
        self.sld = self._sem("sld")
        self.cc = self._sem("cc")
        self.adl = self._sem("adl")
        self._uid = 0
        self._wait_memo = {}

    def _sem(self, name):
        h = self.nc.ctx.enter_context(self.nc.semaphore(name))
        class Sem:
            pass
        s = Sem()
        s.h = h
        s.n = 0
        return s

    def uid(self, p):
        self._uid += 1
        return f"{p}_{self._uid}"

    def sb(self, name, shape, dtype, stk=None):
        ctx = stk if stk is not None else self.nc.ctx
        return ctx.enter_context(self.nc.sbuf_tensor(self.uid(name), shape, dtype))

    def psum(self, name, shape, dtype=F32, stk=None):
        ctx = stk if stk is not None else self.nc.ctx
        return ctx.enter_context(self.nc.psum_tensor(self.uid(name), shape, dtype))

    def dram(self, name, shape, dtype, kind="Internal", addr_space="Local"):
        return self.nc.dram_tensor(name, shape, dtype, kind=kind, addr_space=addr_space).ap()

    def _waits(self, eng, skip=None):
        memo = self._wait_memo.setdefault(eng.engine, {})
        sems = list(self.evs.values()) + [self.gld, self.sld, self.cc, self.adl]
        for s in sems:
            if s.n and s is not skip and memo.get(id(s), -1) < s.n:
                eng.wait_ge(s.h, s.n)
                memo[id(s)] = s.n

    def _expl_waits(self, eng, waits):
        memo = self._wait_memo.setdefault(eng.engine, {})
        for s, n in waits:
            if s is not None and n and memo.get(id(s), -1) < n:
                eng.wait_ge(s.h, n)
                memo[id(s)] = n

    def step(self, engname, f, waits="all"):
        sem = self.evs[engname]

        def body(eng):
            if waits == "all" or FORCE_SERIAL:
                self._waits(eng)
            else:
                self._expl_waits(eng, waits)
            last = f(eng)
            if last is not None:
                last.then_inc(sem.h, 1)
                sem.n += 1
        getattr(self.blk, engname)(body)
        return (sem, sem.n)

    def load(self, engname, pairs, waits="all"):
        sem = {"gpsimd": self.gld, "sync": self.sld}.get(engname)
        if sem is None:
            sem = self.adl

        def body(eng):
            if waits == "all" or FORCE_SERIAL:
                self._waits(eng)
            else:
                self._expl_waits(eng, waits)
            for out, in_ in pairs:
                eng.dma_start(out=out, in_=in_).then_inc(sem.h, 16)
                sem.n += 16
        getattr(self.blk, engname)(body)
        return sem.n

    def collective(self, kind, op, ins, outs):
        def body(gp):
            self._waits(gp)
            gp.collective_compute(kind, op, replica_groups=[list(range(NCORE))],
                                  ins=ins, outs=outs).then_inc(self.cc.h, 1)
            self.cc.n += 1
        self.blk.gpsimd(body)

    def barrier(self):
        for n in ("vector", "scalar", "tensor", "gpsimd", "sync"):
            def body(eng):
                self._waits(eng)
            getattr(self.blk, n)(body)

    def finish(self, engname="sync"):
        def body(eng):
            self._waits(eng)
        getattr(self.blk, engname)(body)



def emit_w2prep(k: KB, w2t, h128_in, out_view):
    FREE = 64 * DW
    stk0 = ExitStack()
    a_bf = k.sb("w2a", [128, FREE], BF16, stk=stk0)
    b_f = k.sb("w2b", [128, FREE], F32, stk=stk0)
    c_f = k.sb("w2c", [128, FREE], F32, stk=stk0)
    h128 = k.sb("h128", [128, 128], BF16, stk=stk0)
    pmm = k.psum("w2p", [128, 1024], stk=stk0)

    src = w2t.rearrange("(u v) d -> u v d", u=128)

    def ld(gp):
        pid = k.nc.partition_id([gp.engine])
        col0 = pid * DW
        gp.dma_start(out=a_bf[:].rearrange("p (v d) -> p v d", v=64),
                     in_=src[:, :, bass.ds(col0, DW)]).then_inc(k.gld.h, 16)
        k.gld.n += 16
        gp.dma_start(out=h128[:], in_=h128_in).then_inc(k.gld.h, 16)
        k.gld.n += 16
    k.blk.gpsimd(ld)

    NMM = FREE // 512
    for i in range(0, NMM, 2):
        def mm(t, i=i):
            t.matmul(pmm[:, 0:512], h128[:], a_bf[:, i * 512:(i + 1) * 512],
                     start=True, stop=True)
            return t.matmul(pmm[:, 512:1024], h128[:], a_bf[:, (i + 1) * 512:(i + 2) * 512],
                            start=True, stop=True)
        k.step("tensor", mm)

        def ev(s, i=i):
            s.copy(b_f[:, i * 512:(i + 1) * 512], pmm[:, 0:512])
            return s.copy(b_f[:, (i + 1) * 512:(i + 2) * 512], pmm[:, 512:1024])
        k.step("scalar", ev)

    def bfly(v):
        cur, nxt = b_f, c_f
        for stg in range(6):
            h = 1 << stg
            ngrp = 64 // (2 * h)
            av = cur[:].rearrange("p (g two h d) -> p g two h d", g=ngrp, two=2, h=h)
            nv = nxt[:].rearrange("p (g two h d) -> p g two h d", g=ngrp, two=2, h=h)
            v.tensor_add(nv[:, :, 0], av[:, :, 0], av[:, :, 1])
            v.tensor_sub(nv[:, :, 1], av[:, :, 0], av[:, :, 1])
            cur, nxt = nxt, cur
        return v.tensor_copy(a_bf[:], cur[:])
    k.step("vector", bfly)

    k.load("sync", [(out_view.rearrange("(u v) d -> u (v d)", u=128), a_bf[:])])
    stk0.close()


class LnEnv:
    def __init__(self, k: KB):
        self.xb = k.sb("lnxb", [128, 4 * GT], BF16)
        self.xsq = k.sb("lnxq", [128, 4 * GT], BF16)
        self.rows = k.sb("lnrw", [1, 2048], F32)
        self.rowsb = k.sb("lnrb", [1, 1024], BF16)
        self.scr = k.sb("lnsc", [128, GT], F32)
        self.onescol = k.sb("ones_c", [128, 1], BF16)
        self.onesrow = k.sb("ones_r", [1, 128], BF16)

        def init(v):
            v.memset(self.onescol[:], 1.0)
            return v.memset(self.onesrow[:], 1.0)
        k.step("vector", init)


ZMARK = (None, 0)


def emit_ln(k: KB, ln: LnEnv, xg, xlng, pA, pB):
    m_cast = {}
    m_sq = {}
    m_mm = {}
    for q in range(ND // 4):
        dts = list(range(q * 4, q * 4 + 4))

        def cst(v, dts=dts):
            last = None
            for dt in dts:
                last = v.tensor_copy(ln.xb[:, (dt % 4) * GT:(dt % 4 + 1) * GT],
                                     xg[:, dt * GT:(dt + 1) * GT])
            return last
        m_cast[q] = k.step("vector", cst,
                           waits="all" if (q == 0 or SER_LN)
                           else [m_mm.get(q - 1, ZMARK)])

        def sq(s, dts=dts):
            last = None
            for dt in dts:
                last = s.activation(ln.xsq[:, (dt % 4) * GT:(dt % 4 + 1) * GT],
                                    xg[:, dt * GT:(dt + 1) * GT], AF.Square)
            return last
        m_sq[q] = k.step("scalar", sq,
                         waits="all" if (q == 0 or SER_LN)
                         else [m_mm.get(q - 1, ZMARK)])

        def mm(t, dts=dts):
            inst = None
            for dt in dts:
                bsl = slice((dt % 4) * GT, (dt % 4 + 1) * GT)
                t.matmul(pA[0:1, 0:GT], ln.onescol[:], ln.xb[:, bsl],
                         start=(dt == 0), stop=(dt == ND - 1))
                inst = t.matmul(pA[0:1, 512:512 + GT], ln.onescol[:], ln.xsq[:, bsl],
                                start=(dt == 0), stop=(dt == ND - 1))
            return inst
        m_mm[q] = k.step("tensor", mm,
                         waits="all" if SER_LN else [m_cast[q], m_sq[q]])

    def rows1(v):
        mu = ln.rows[0:1, 0:GT]
        msq = ln.rows[0:1, 512:512 + GT]
        var = ln.rows[0:1, 1024:1024 + GT]
        v.tensor_scalar_mul(mu, pA[0:1, 0:GT], 1.0 / D)
        v.tensor_scalar_mul(msq, pA[0:1, 512:512 + GT], 1.0 / D)
        v.tensor_mul(var, mu, mu)
        v.tensor_sub(var, msq, var)
        return v.tensor_scalar_add(var, var, EPS_LN)
    m = k.step("vector", rows1, waits="all" if SER_LN2 else [m_mm[ND // 4 - 1]])

    def rows2(s):
        return s.activation(ln.rows[0:1, 1536:1536 + GT], ln.rows[0:1, 1024:1024 + GT],
                            AF.Sqrt)
    m = k.step("scalar", rows2, waits="all" if SER_LN2 else [m])

    def rows3(v):
        v.reciprocal(ln.rows[0:1, 1024:1024 + GT], ln.rows[0:1, 1536:1536 + GT])
        v.tensor_copy(ln.rowsb[0:1, 0:GT], ln.rows[0:1, 0:GT])
        return v.tensor_copy(ln.rowsb[0:1, 512:512 + GT], ln.rows[0:1, 1024:1024 + GT])
    m = k.step("vector", rows3, waits="all" if SER_LN2 else [m])

    def bcast(t):
        t.matmul(pB[:, 0:GT], ln.onesrow[:], ln.rowsb[0:1, 0:GT], start=True, stop=True)
        return t.matmul(pB[:, 512:512 + GT], ln.onesrow[:], ln.rowsb[0:1, 512:512 + GT],
                        start=True, stop=True)
    m = k.step("tensor", bcast, waits="all" if SER_LN2 else [m])

    def apply(v):
        last = None
        for dt in range(ND):
            sl = slice(dt * GT, (dt + 1) * GT)
            v.tensor_sub(ln.scr[:], xg[:, sl], pB[:, 0:GT])
            last = v.tensor_mul(xlng[:, sl], ln.scr[:], pB[:, 512:512 + GT])
        return last
    return k.step("vector", apply, waits="all" if SER_LN2 else [m])


class Ffn0Env:
    def __init__(self, k: KB, stk=None):
        self.xg = k.sb("f0xg", [128, ND * GT], F32, stk=stk)
        self.xlng = k.sb("f0xl", [128, ND * GT], BF16, stk=stk)
        self.hid = k.sb("f0h", [128, NH0T * GT], BF16, stk=stk)
        self.w1b = k.sb("f0w1", [128, 2 * ND * 128], BF16, stk=stk)
        self.w3b = k.sb("f0w3", [128, 2 * ND * 128], BF16, stk=stk)
        self.w2b = k.sb("f0w2", [128, 2 * 8192], BF16, stk=stk)
        self.hsil = k.sb("f0hs", [128, 2 * GT], F32, stk=stk)
        self.xnew = k.sb("f0xn", [128, 8 * GT], F32, stk=stk)
        self.pA = k.psum("f0pA", [128, 1024], stk=stk)
        self.pB = k.psum("f0pB", [128, 1024], stk=stk)
        self.pO = k.psum("f0pO", [128, 2048], stk=stk)


def emit_ffn0(k: KB, ln: LnEnv, xin, xout, w1t, w3t, w2pairs, cast=True):
    stk0 = ExitStack()
    env = Ffn0Env(k, stk0)
    w1v = w1t.rearrange("(dt p) h -> p dt h", p=128)
    w3v = w3t.rearrange("(dt p) h -> p dt h", p=128)
    prev_mm1 = ZMARK
    k.barrier()

    st_after = 0
    prev_mm2_last = ZMARK
    m_epi = ZMARK
    for g in range(NG):
        gsl = slice(g * GT, (g + 1) * GT)
        k.load("sync", [(env.xg[:].rearrange("p (dt t) -> p dt t", dt=ND),
                         xin[:, gsl].rearrange("(dt p) t -> p dt t", p=128))])
        mark_apply = emit_ln(k, ln, env.xg, env.xlng, env.pA, env.pB)
        m_mm = {}
        m_mul = {}
        m_silu = {}
        wl_after = {}

        for ht in range(NH0T):
            buf = ht % 2
            hsl = slice(ht * 128, (ht + 1) * 128)
            wsl = slice(buf * 2048, buf * 2048 + 2048)
            lw = [m_mm.get(ht - 2, prev_mm1)]
            if cast:
                wl_after[ht] = k.load("gpsimd", [
                    (env.w1b[:, wsl].rearrange("p (dt h) -> p dt h", dt=ND),
                     w1v[:, :, hsl]),
                    (env.w3b[:, wsl].rearrange("p (dt h) -> p dt h", dt=ND),
                     w3v[:, :, hsl])], waits="all" if SER_P1 else lw)
            else:
                k.load("sync", [(env.w1b[:, wsl].rearrange("p (dt h) -> p dt h", dt=ND),
                                 w1v[:, :, hsl])], waits="all" if SER_P1 else lw)
                wl3 = k.load("scalar", [(env.w3b[:, wsl].rearrange("p (dt h) -> p dt h", dt=ND),
                                         w3v[:, :, hsl])], waits="all" if SER_P1 else lw)
                wl_after[ht] = (k.sld.n, wl3)

            bank = env.pA if buf == 0 else env.pB

            def mm1(t, buf=buf, bank=bank):
                for dt in range(ND):
                    xs = env.xlng[:, dt * GT:(dt + 1) * GT]
                    t.matmul(bank[:, 0:512], env.w1b[:, buf * 2048 + dt * 128:buf * 2048 + (dt + 1) * 128],
                             xs, start=(dt == 0), stop=(dt == ND - 1))
                for dt in range(ND):
                    xs = env.xlng[:, dt * GT:(dt + 1) * GT]
                    inst = t.matmul(bank[:, 512:1024], env.w3b[:, buf * 2048 + dt * 128:buf * 2048 + (dt + 1) * 128],
                                    xs, start=(dt == 0), stop=(dt == ND - 1))
                return inst
            if cast:
                ws = [(k.gld, wl_after[ht]), mark_apply,
                      m_mul.get(ht - 2, ZMARK), m_silu.get(ht - 2, ZMARK)]
            else:
                ws = [(k.sld, wl_after[ht][0]), (k.adl, wl_after[ht][1]), mark_apply,
                      m_mul.get(ht - 2, ZMARK), m_silu.get(ht - 2, ZMARK)]
            m_mm[ht] = k.step("tensor", mm1, waits="all" if SER_P1 else ws)

            def silu(s, buf=buf, bank=bank):
                return s.activation(env.hsil[:, buf * GT:(buf + 1) * GT],
                                    bank[:, 0:512], AF.Silu)
            m_silu[ht] = k.step("scalar", silu,
                                waits="all" if SER_P1
                                else [m_mm[ht], m_mul.get(ht - 2, ZMARK)])

            def mul(v, ht=ht, buf=buf, bank=bank):
                return v.tensor_mul(env.hid[:, ht * GT:(ht + 1) * GT],
                                    bank[:, 512:1024], env.hsil[:, buf * GT:(buf + 1) * GT])
            m_mul[ht] = k.step("vector", mul,
                               waits="all" if SER_P1 else [m_silu[ht], m_mm[ht]])

        prev_mm1 = m_mm[NH0T - 1]

        for rnd in range(2):
            banks = [env.pO[:, 0:512], env.pO[:, 512:1024], env.pO[:, 1024:1536],
                     env.pO[:, 1536:2048], env.pA[:, 0:512], env.pA[:, 512:1024],
                     env.pB[:, 0:512], env.pB[:, 512:1024]]
            m_mm2 = {}
            w2_after = {}
            for sb in range(8):
                buf = sb % 2
                w2view = env.w2b[:, buf * 8192:(buf + 1) * 8192]
                w2_after[sb] = k.load("sync", w2pairs(w2view, sb, rnd),
                                      waits="all" if SER_P2
                                      else [m_mm2.get(sb - 2, prev_mm2_last)])

                def mm2(t, sb=sb, buf=buf, banks=banks):
                    inst = None
                    for hh in range(8):
                        ht = sb * 8 + hh
                        for dtt in range(8):
                            inst = t.matmul(banks[dtt],
                                            env.w2b[:, buf * 8192 + hh * 1024 + dtt * 128:
                                                    buf * 8192 + hh * 1024 + (dtt + 1) * 128],
                                            env.hid[:, ht * GT:(ht + 1) * GT],
                                            start=(ht == 0), stop=(ht == NH0T - 1))
                    return inst
                ws = [(k.sld, w2_after[sb]), m_mul[NH0T - 1], m_epi]
                m_mm2[sb] = k.step("tensor", mm2, waits="all" if SER_P2 else ws)

            prev_mm2_last = m_mm2[7]
            em = ZMARK
            for dtt in range(8):
                dt = rnd * 8 + dtt

                def epi(v, dtt=dtt, dt=dt, banks=banks):
                    return v.scalar_tensor_tensor(
                        env.xnew[:, dtt * GT:(dtt + 1) * GT], banks[dtt], W2SCALE,
                        env.xg[:, dt * GT:(dt + 1) * GT], ALU.mult, ALU.add)
                em = k.step("vector", epi,
                            waits="all" if SER_P2
                            else [m_mm2[7], (k.sld, st_after)])
            m_epi = em
            st_after = k.load(
                "sync",
                [(xout[rnd * 1024:(rnd + 1) * 1024, gsl].rearrange("(dtt p) t -> p dtt t", p=128),
                  env.xnew[:].rearrange("p (dtt t) -> p dtt t", dtt=8))],
                waits="all" if SER_P2 else [m_epi])
    stk0.close()


class Ffn1Env:
    def __init__(self, k: KB, stk=None):
        self.xg = k.sb("f1xg", [128, ND * GT], F32, stk=stk)
        self.xlng = k.sb("f1xl", [128, ND * GT], BF16, stk=stk)
        self.hid = k.sb("f1h", [128, NH1T * GT], BF16, stk=stk)
        self.wgb = k.sb("f1wg", [128, 2 * ND * 128], BF16, stk=stk)
        self.wub = k.sb("f1wu", [128, 2 * ND * 128], BF16, stk=stk)
        self.wdb = k.sb("f1wd", [128, 2 * 8192], BF16, stk=stk)
        self.hsil = k.sb("f1hs", [128, 2 * GT], F32, stk=stk)
        self.xnew = k.sb("f1xn", [128, 8 * GT], F32, stk=stk)
        self.sgu = k.sb("f1sg", [128, 2 * NH1T], F32, stk=stk)
        self.sdh = k.sb("f1sd", [128, ND], F32, stk=stk)
        self.hrow = k.sb("f1hr", [1, GT], F32, stk=stk)
        self.hrowb = k.sb("f1hb", [1, GT], BF16, stk=stk)
        self.pA = k.psum("f1pA", [128, 1024], stk=stk)
        self.pB = k.psum("f1pB", [128, 1024], stk=stk)
        self.pO = k.psum("f1pO", [128, 2048], stk=stk)


def emit_ffn1(k: KB, ln: LnEnv, xin, xout, wgt, wut, wdt, sg, su, sd):
    stk0 = ExitStack()
    env = Ffn1Env(k, stk0)
    wgv = wgt.rearrange("(dt p) h -> p dt h", p=128)
    wuv = wut.rearrange("(dt p) h -> p dt h", p=128)
    NSB = (NH1T + 7) // 8
    prev_mm1 = ZMARK
    k.barrier()

    k.load("sync", [(env.sgu[:, 0:NH1T], sg), (env.sgu[:, NH1T:2 * NH1T], su),
                    (env.sdh[:], sd)])

    def halfsd(v):
        return v.tensor_scalar_mul(env.sdh[:], env.sdh[:], 0.5)
    k.step("vector", halfsd)

    st_after = 0
    prev_mm2_last = ZMARK
    m_epi = ZMARK
    for g in range(NG):
        gsl = slice(g * GT, (g + 1) * GT)
        k.load("sync", [(env.xg[:].rearrange("p (dt t) -> p dt t", dt=ND),
                         xin[:, gsl].rearrange("(dt p) t -> p dt t", p=128))])
        mark_apply = emit_ln(k, ln, env.xg, env.xlng, env.pA, env.pB)

        def xsum(t):
            inst = None
            for dt in range(ND):
                inst = t.matmul(env.pB[0:1, 0:GT], ln.onescol[:],
                                env.xlng[:, dt * GT:(dt + 1) * GT],
                                start=(dt == 0), stop=(dt == ND - 1))
            return inst
        m_xs = k.step("tensor", xsum, waits=[mark_apply])

        def xsrow(v):
            v.tensor_scalar_mul(env.hrow[:], env.pB[0:1, 0:GT], -128.0)
            return v.tensor_copy(env.hrowb[:], env.hrow[:])
        m_xsr = k.step("vector", xsrow, waits=[m_xs])

        m_mm = {}
        m_mul = {}
        m_silu = {}
        for ht in range(NH1T):
            buf = ht % 2
            hsl = slice(ht * 128, (ht + 1) * 128)
            wsl = slice(buf * 2048, buf * 2048 + 2048)
            lw = [m_mm.get(ht - 2, prev_mm1)]
            wl = k.load("gpsimd", [
                (env.wgb[:, wsl].rearrange("p (dt h) -> p dt h", dt=ND),
                 wgv[:, :, hsl]),
                (env.wub[:, wsl].rearrange("p (dt h) -> p dt h", dt=ND),
                 wuv[:, :, hsl])], waits=lw)

            bank = env.pA if buf == 0 else env.pB

            def mm1(t, buf=buf, bank=bank):
                t.matmul(bank[:, 0:512], ln.onesrow[:], env.hrowb[0:1, :],
                         start=True, stop=False)
                t.matmul(bank[:, 512:1024], ln.onesrow[:], env.hrowb[0:1, :],
                         start=True, stop=False)
                for dt in range(ND):
                    xs = env.xlng[:, dt * GT:(dt + 1) * GT]
                    t.matmul(bank[:, 0:512], env.wgb[:, buf * 2048 + dt * 128:buf * 2048 + (dt + 1) * 128],
                             xs, start=False, stop=(dt == ND - 1))
                for dt in range(ND):
                    xs = env.xlng[:, dt * GT:(dt + 1) * GT]
                    inst = t.matmul(bank[:, 512:1024], env.wub[:, buf * 2048 + dt * 128:buf * 2048 + (dt + 1) * 128],
                                    xs, start=False, stop=(dt == ND - 1))
                return inst
            ws = [(k.gld, wl), m_xsr, mark_apply,
                  m_mul.get(ht - 2, ZMARK), m_silu.get(ht - 2, ZMARK)]
            m_mm[ht] = k.step("tensor", mm1, waits=ws)

            def silu(s, ht=ht, buf=buf, bank=bank):
                return s.activation(env.hsil[:, buf * GT:(buf + 1) * GT],
                                    bank[:, 0:512], AF.Silu,
                                    scale=env.sgu[:, ht:ht + 1])
            m_silu[ht] = k.step("scalar", silu,
                                waits=[m_mm[ht], m_mul.get(ht - 2, ZMARK)])

            def mul(v, ht=ht, buf=buf, bank=bank):
                return v.scalar_tensor_tensor(env.hid[:, ht * GT:(ht + 1) * GT],
                                              bank[:, 512:1024],
                                              env.sgu[:, NH1T + ht:NH1T + ht + 1],
                                              env.hsil[:, buf * GT:(buf + 1) * GT],
                                              ALU.mult, ALU.mult)
            m_mul[ht] = k.step("vector", mul, waits=[m_silu[ht], m_mm[ht]])

        prev_mm1 = m_mm[NH1T - 1]

        def hsum(t):
            inst = None
            for ht in range(NH1T):
                inst = t.matmul(env.pB[0:1, 0:GT], ln.onescol[:],
                                env.hid[:, ht * GT:(ht + 1) * GT],
                                start=(ht == 0), stop=(ht == NH1T - 1))
            return inst
        m_hs = k.step("tensor", hsum, waits=[m_mul[NH1T - 1], m_xsr])

        def hrow(v):
            v.tensor_scalar_mul(env.hrow[:], env.pB[0:1, 0:GT], -128.0)
            return v.tensor_copy(env.hrowb[:], env.hrow[:])
        m_hr = k.step("vector", hrow, waits=[m_hs])

        for rnd in range(2):
            banks = [env.pO[:, 0:512], env.pO[:, 512:1024], env.pO[:, 1024:1536],
                     env.pO[:, 1536:2048], env.pA[:, 0:512], env.pA[:, 512:1024],
                     env.pB[:, 0:512], env.pB[:, 512:1024]]
            m_mm2 = {}
            for sb in range(NSB):
                buf = sb % 2
                nhh = min(8, NH1T - sb * 8)
                wl2 = k.load("gpsimd", [
                    (env.wdb[:, buf * 8192:buf * 8192 + nhh * 1024].rearrange(
                        "p (hh o) -> p hh o", hh=nhh),
                     wdt[sb * 1024:sb * 1024 + nhh * 128,
                         rnd * 1024:(rnd + 1) * 1024].rearrange("(hh p) o -> p hh o", p=128))],
                    waits=[m_mm2.get(sb - 2, prev_mm2_last)])

                def mm2(t, sb=sb, buf=buf, nhh=nhh, banks=banks):
                    inst = None
                    if sb == 0:
                        for dtt in range(8):
                            inst = t.matmul(banks[dtt], ln.onesrow[:], env.hrowb[0:1, :],
                                            start=True, stop=False)
                    for hh in range(nhh):
                        ht = sb * 8 + hh
                        for dtt in range(8):
                            inst = t.matmul(banks[dtt],
                                            env.wdb[:, buf * 8192 + hh * 1024 + dtt * 128:
                                                    buf * 8192 + hh * 1024 + (dtt + 1) * 128],
                                            env.hid[:, ht * GT:(ht + 1) * GT],
                                            start=False, stop=(ht == NH1T - 1))
                    return inst
                ws = [(k.gld, wl2), m_hr, m_mul[NH1T - 1], m_epi]
                m_mm2[sb] = k.step("tensor", mm2, waits=ws)

            prev_mm2_last = m_mm2[NSB - 1]
            em = ZMARK
            for dtt in range(8):
                dt = rnd * 8 + dtt

                def epi(v, dtt=dtt, dt=dt, banks=banks):
                    return v.scalar_tensor_tensor(
                        env.xnew[:, dtt * GT:(dtt + 1) * GT], banks[dtt],
                        env.sdh[:, dt:dt + 1],
                        env.xg[:, dt * GT:(dt + 1) * GT], ALU.mult, ALU.add)
                em = k.step("vector", epi,
                            waits=[m_mm2[NSB - 1], (k.sld, st_after)])
            m_epi = em
            st_after = k.load(
                "sync",
                [(xout[rnd * 1024:(rnd + 1) * 1024, gsl].rearrange("(dtt p) t -> p dtt t", p=128),
                  env.xnew[:].rearrange("p (dtt t) -> p dtt t", dtt=8))],
                waits=[m_epi])
    stk0.close()


def emit_attn(k: KB, ln: LnEnv, xin, xout, wqkvt, woutt, i128bf, trimask, contrib, ag, contrib_o, ago):
    nc = k.nc
    NHH = 16

    with ExitStack() as stk:
        xg = k.sb("atxg", [128, ND * GT], F32, stk=stk)
        xlng = k.sb("atxl", [128, ND * GT], BF16, stk=stk)
        wblk = k.sb("atwb", [128, 2 * ND * 512], BF16, stk=stk)
        qko = k.sb("atqk", [128, 2 * 512], BF16, stk=stk)
        pA = k.psum("atpA", [128, 1024], stk=stk)
        pB = k.psum("atpB", [128, 1024], stk=stk)
        wqv = wqkvt.rearrange("(dt p) o -> p dt o", p=128)
        k.barrier()
        m_mm = {}
        m_evc = {}
        st_q = {}

        for g in range(NG):
            gsl = slice(g * GT, (g + 1) * GT)
            k.load("sync", [(xg[:].rearrange("p (dt t) -> p dt t", dt=ND),
                             xin[:, gsl].rearrange("(dt p) t -> p dt t", p=128))])
            mark_apply = emit_ln(k, ln, xg, xlng, pA, pB)

            for ot in range(32):
                buf = ot % 2
                osl = slice(ot * 128, (ot + 1) * 128)
                wl = k.load("gpsimd",
                            [(wblk[:, buf * 2048:buf * 2048 + 2048].rearrange(
                                "p (dt h) -> p dt h", dt=ND), wqv[:, :, osl])],
                            waits=[m_mm.get(("qk", g, ot - 2),
                                            m_mm.get(("v", g - 1, 2, 3), ZMARK))])
                bank = pA[:, 0:512] if buf == 0 else pA[:, 512:1024]

                def mmqk(t, buf=buf, bank=bank):
                    inst = None
                    for dt in range(ND):
                        inst = t.matmul(bank, wblk[:, buf * 2048 + dt * 128:buf * 2048 + (dt + 1) * 128],
                                        xlng[:, dt * GT:(dt + 1) * GT],
                                        start=(dt == 0), stop=(dt == ND - 1))
                    return inst
                m_mm[("qk", g, ot)] = k.step(
                    "tensor", mmqk,
                    waits=[(k.gld, wl), mark_apply,
                           m_evc.get(("qk", g, ot - 2), m_evc.get(("last", g - 1), ZMARK))])

                def ev(v, buf=buf, bank=bank):
                    return v.tensor_copy(qko[:, buf * 512:(buf + 1) * 512], bank)
                m_evc[("qk", g, ot)] = k.step(
                    "vector", ev,
                    waits=[m_mm[("qk", g, ot)],
                           (k.sld, st_q.get(("qk", g, ot - 2),
                                            st_q.get(("v", g - 1, 3, 2 + (ot % 2)), 0)))])
                st_q[("qk", g, ot)] = k.load(
                    "sync", [(contrib[ot * 128:(ot + 1) * 128, gsl],
                              qko[:, buf * 512:(buf + 1) * 512])],
                    waits=[m_evc[("qk", g, ot)]])

            prev_ev = m_evc[("qk", g, 31)]
            for vb in range(4):
                buf = vb % 2
                vsl = slice(4096 + vb * 512, 4096 + (vb + 1) * 512)
                wl = k.load("gpsimd",
                            [(wblk[:, buf * 8192:(buf + 1) * 8192].rearrange(
                                "p (dt h) -> p dt h", dt=ND), wqv[:, :, vsl])],
                            waits=[m_mm.get(("v", g, vb - 2, 3),
                                            m_mm.get(("qk", g, 31), ZMARK))])
                for ttl in range(4):
                    pbuf = ttl % 2
                    bank = pB[:, 0:512] if pbuf == 0 else pB[:, 512:1024]

                    def mmv(t, ttl=ttl, buf=buf, bank=bank):
                        inst = None
                        for dt in range(ND):
                            inst = t.matmul(bank,
                                            xlng[:, dt * GT + ttl * 128:dt * GT + (ttl + 1) * 128],
                                            wblk[:, buf * 8192 + dt * 512:buf * 8192 + (dt + 1) * 512],
                                            start=(dt == 0), stop=(dt == ND - 1))
                        return inst
                    m_mm[("v", g, vb, ttl)] = k.step(
                        "tensor", mmv,
                        waits=[(k.gld, wl), mark_apply,
                               m_evc.get(("v", g, vb, ttl - 2),
                                         m_evc.get(("v", g, vb - 1, 2 + ttl), prev_ev))])

                    def ev(v, buf2=(vb * 4 + ttl) % 2, bank=bank):
                        return v.tensor_copy(qko[:, buf2 * 512:(buf2 + 1) * 512], bank)
                    m_evc[("v", g, vb, ttl)] = k.step(
                        "vector", ev,
                        waits=[m_mm[("v", g, vb, ttl)],
                               (k.sld, st_q.get(("v", g, vb, ttl - 2),
                                                st_q.get(("v", g, vb - 1, 2 + ttl),
                                                         st_q.get(("qk", g, 30 + ((vb * 4 + ttl) % 2)), 0))))])
                    st_q[("v", g, vb, ttl)] = k.load(
                        "sync", [(contrib[4096 + g * GT + ttl * 128:
                                          4096 + g * GT + (ttl + 1) * 128,
                                          vb * 512:(vb + 1) * 512],
                                  qko[:, ((vb * 4 + ttl) % 2) * 512:(((vb * 4 + ttl) % 2) + 1) * 512])],
                        waits=[m_evc[("v", g, vb, ttl)]])
            m_evc[("last", g)] = m_evc[("v", g, 3, 3)]

    if ATTN_UPTO < 1:
        return
    if ATTN_DBG:
        dbg_contrib = k.dram("dbg_contrib", [3 * D, TPC], BF16, kind="ExternalOutput")
        k.load("sync", [(dbg_contrib, contrib)])
    k.collective("AllGather", ALU.bypass, [contrib], [ag])

    kc = k.dram(k.uid("at_kc"), [2048, 2048], BF16)
    qc = k.dram(k.uid("at_qc"), [2048, 2048], BF16)
    vc = k.dram(k.uid("at_vc"), [TOK, 256], BF16)

    def canon(gp):
        pid = k.nc.partition_id([gp.engine])
        agk = ag.rearrange("(brl sec hh p) t -> brl sec hh p t",
                           brl=8, sec=3, hh=16)
        qdst = qc.rearrange("(brl sp) t -> brl sp t", brl=8)
        kdst = kc.rearrange("(brl sp) t -> brl sp t", brl=8)
        src_q = agk[:, 0, bass.ds(pid * 2, 2)].rearrange("brl s p t -> brl (s p) t")
        src_k = agk[:, 1, bass.ds(pid * 2, 2)].rearrange("brl s p t -> brl (s p) t")
        gp.dma_start(out=qdst, in_=src_q).then_inc(k.gld.h, 16)
        k.gld.n += 16
        gp.dma_start(out=kdst, in_=src_k).then_inc(k.gld.h, 16)
        k.gld.n += 16
        if ATTN_UPTO >= 2.5:
            agv = ag.rearrange("(bb rlt sec tok) (hp c) -> bb rlt sec tok hp c",
                               bb=2, rlt=4, sec=3, hp=8)
            vdst = vc.rearrange("(bb rlt tok) c -> bb rlt tok c", bb=2, rlt=4)
            for bb in range(2):
                for rlt in range(4):
                    gp.dma_start(out=vdst[bb, rlt],
                                 in_=agv[bb, rlt, 2, :, bass.ds(pid, 1), :].squeeze(axis=1)
                                 ).then_inc(k.gld.h, 16)
                    k.gld.n += 16
    if ATTN_UPTO < 2:
        return
    k.step("gpsimd", canon)

    if ATTN_DBG:
        dbg_kc = k.dram("dbg_kc", [2048, 2048], BF16, kind="ExternalOutput")
        dbg_qc = k.dram("dbg_qc", [2048, 2048], BF16, kind="ExternalOutput")
        dbg_vc = k.dram("dbg_vc", [TOK, 256], BF16, kind="ExternalOutput")
        k.load("sync", [(dbg_kc, kc), (dbg_qc, qc), (dbg_vc, vc)])
    if ATTN_UPTO < 3:
        return
    with ExitStack() as stk:
        Kf = k.sb("atK", [128, S], BF16, stk=stk)
        Qf = k.sb("atQ", [128, S], BF16, stk=stk)
        vt = k.sb("atV", [128, 64 * 129], BF16, stk=stk)
        ptb = k.sb("atP", [128, 2048], BF16, stk=stk)
        sq = k.sb("atsq", [128, 512], BF16, stk=stk)
        srow = k.sb("atsr", [1, 8], F32, stk=stk)
        chm = k.sb("atcm", [1, 32], F32, stk=stk)
        srowb = k.sb("atsb", [1, 1], BF16, stk=stk)
        negsh = k.sb("atns", [128, 1], F32, stk=stk)
        scl128 = k.sb("atsc", [128, 1], F32, stk=stk)
        ob = k.sb("atob", [128, 256], BF16, stk=stk)
        rl = k.sb("atrl", [128, 8], F32, stk=stk)
        pS = k.psum("atpS", [128, 2048], stk=stk)
        pacc = k.psum("atpa", [128, 1024], stk=stk)
        pacc2 = k.psum("atp2", [128, 512], stk=stk)
        pn = k.psum("atpn", [128, 512], stk=stk)
        accs = [pacc[:, j * 132:j * 132 + 129] for j in range(3)] + \
               [pacc[:, 512 + j * 132:512 + j * 132 + 129] for j in range(3)] + \
               [pacc2[:, j * 132:j * 132 + 129] for j in range(2)]
        vtv = vt[:].rearrange("p (c w) -> p c w", w=129)

        nslots = 1 if ATTN_UPTO < 5 else 4
        for slot in range(nslots):
            b = slot // 2
            s = slot % 2
            vsl = vc[b * 8192:(b + 1) * 8192, s * 128:(s + 1) * 128].rearrange(
                "(c p) w -> p c w", p=128)
            qkp = [(Qf[:, rl * 2048:(rl + 1) * 2048],
                    qc[b * 1024 + rl * 256 + s * 128:b * 1024 + rl * 256 + (s + 1) * 128, :])
                   for rl in range(4)]
            qkp += [(Kf[:, rl * 2048:(rl + 1) * 2048],
                     kc[b * 1024 + rl * 256 + s * 128:b * 1024 + rl * 256 + (s + 1) * 128, :])
                    for rl in range(4)]
            qkp.append((vtv[:, :, 0:128], vsl))
            k.load("sync", qkp)

            def onesv(v):
                v.memset(vtv[:, :, 128:129], 1.0)
                v.memset(scl128[:], 1.0 / float(np.sqrt(128.0)))
                return v.memset(srow[:], 0.0)
            k.step("vector", onesv)

            for which, buf in ((0, Kf), (1, Qf)):
                for i in range(16):
                    def sqr(s, buf=buf, i=i):
                        return s.activation(sq[:], buf[:, i * 512:(i + 1) * 512], AF.Square)
                    k.step("scalar", sqr)

                    def nmm(t):
                        return t.matmul(pn[0:1, 0:512], ln.onescol[:], sq[:],
                                        start=True, stop=True)
                    k.step("tensor", nmm)

                    def nmax(v, which=which, i=i):
                        return v.reduce_max(chm[0:1, which * 16 + i:which * 16 + i + 1],
                                            pn[0:1, 0:512], mybir.AxisListType.X)
                    k.step("vector", nmax)

            def nfin(v):
                v.reduce_max(srow[0:1, 0:1], chm[0:1, 0:16], mybir.AxisListType.X)
                return v.reduce_max(srow[0:1, 1:2], chm[0:1, 16:32], mybir.AxisListType.X)
            k.step("vector", nfin)

            def shrow(s):
                s.activation(srow[0:1, 2:3], srow[0:1, 0:1], AF.Sqrt)
                return s.activation(srow[0:1, 3:4], srow[0:1, 1:2], AF.Sqrt)
            k.step("scalar", shrow)

            def shrow2a(v):
                return v.tensor_mul(srow[0:1, 5:6], srow[0:1, 2:3], srow[0:1, 3:4])
            k.step("vector", shrow2a)

            def shrow2b(v):
                return v.tensor_scalar(srowb[:], srow[0:1, 5:6],
                                       1.0 / float(np.sqrt(128.0)), None, ALU.mult)
            k.step("vector", shrow2b)

            def shb(t):
                return t.matmul(pn[:, 0:1], ln.onesrow[:], srowb[0:1, 0:1],
                                start=True, stop=True)
            k.step("tensor", shb)

            def shneg(v):
                return v.tensor_scalar_mul(negsh[:], pn[:, 0:1], -1.0)
            k.step("vector", shneg)

            if ATTN_DBG:
                dbg_sh = k.dram(k.uid("dbg_sh"), [128, 1], F32, kind="ExternalOutput")
                dbg_srow = k.dram(k.uid("dbg_srow"), [1, 8], F32, kind="ExternalOutput")
                k.load("sync", [(dbg_sh, negsh[:]), (dbg_srow, srow[:])])

            if ATTN_UPTO < 4:
                continue
            m_st = {}
            m_expo = {}
            m_pv = {}
            m_fin1 = {}
            m_fin2 = {}
            ob_st = {}
            m_fin_last = ZMARK
            ci = 0
            for qg in range(8):
                nch = 8 * qg + 8
                chunks = []
                for c in range(nch):
                    jlo = max(0, c - 8 * qg)
                    chunks.append((ci + c, c, jlo, 1024 - jlo * 128,
                                   qg * 1024 + jlo * 128))

                def emit_st(t, c, W, q0, jlo, par, qg=qg):
                    inst = None
                    diag = c >= 8 * qg
                    off = 0
                    base = par * 1024
                    while off < W:
                        seg = min(512, W - off)
                        inst = t.matmul(pS[:, base + off:base + off + seg],
                                        Kf[:, c * 128:(c + 1) * 128],
                                        Qf[:, q0 + off:q0 + off + seg],
                                        start=True,
                                        stop=not (diag and off == 0))
                        off += seg
                    if diag:
                        inst = t.matmul(pS[:, base:base + 128], i128bf, trimask,
                                        start=False, stop=True)
                    return inst

                for (ii, c, jlo, W, q0) in chunks:
                    par = ii % 2
                    if c == 0:
                        m_st[ii] = k.step(
                            "tensor",
                            lambda t, c=c, W=W, q0=q0, jlo=jlo, par=par: emit_st(t, c, W, q0, jlo, par),
                            waits=[m_expo.get(ii - 2, ZMARK), m_fin_last])

                    def expo(s, W=W, par=par):
                        return s.activation(ptb[:, par * 1024:par * 1024 + W],
                                            pS[:, par * 1024:par * 1024 + W], AF.Exp,
                                            bias=negsh[:], scale=scl128[:])
                    m_expo[ii] = k.step("scalar", expo,
                                        waits=[m_st[ii], m_pv.get(ii - 2, ZMARK)])

                    if c + 1 < nch:
                        (ii2, c2, jlo2, W2, q02) = chunks[c + 1]
                        m_st[ii2] = k.step(
                            "tensor",
                            lambda t, c=c2, W=W2, q0=q02, jlo=jlo2, par=ii2 % 2: emit_st(t, c, W, q0, jlo, par),
                            waits=[m_expo.get(ii2 - 2, ZMARK), m_fin_last])

                    def pv(t, c=c, jlo=jlo, qg=qg, par=par):
                        inst = None
                        for j in range(jlo, 8):
                            inst = t.matmul(accs[j],
                                            ptb[:, par * 1024 + (j - jlo) * 128:par * 1024 + (j - jlo + 1) * 128],
                                            vtv[:, c, :],
                                            start=(c == 0 and j in (0, 3, 6)),
                                            stop=(c == 8 * qg + j),
                                            skip_group_check=True)
                        return inst
                    m_pv[ii] = k.step("tensor", pv,
                                      waits=[m_expo[ii], m_fin_last])
                ci += nch

                for j in range(8):
                    def fin1(v, j=j):
                        return v.reciprocal(rl[:, j:j + 1], accs[j][:, 128:129])
                    m_fin1[j] = k.step("vector", fin1,
                                       waits=[m_pv[ci - 1], m_fin2.get(j - 2, ZMARK)])

                    def fin2(s, j=j):
                        return s.activation(ob[:, (j % 2) * 128:(j % 2) * 128 + 128],
                                            accs[j][:, 0:128], AF.Copy,
                                            scale=rl[:, j:j + 1])
                    m_fin2[j] = k.step("scalar", fin2,
                                       waits=[m_fin1[j], (k.sld, ob_st.get(j - 2, 0))])
                    ob_st[j] = k.load(
                        "sync",
                        [(contrib_o[slot * S + (qg * 8 + j) * 128:
                                    slot * S + (qg * 8 + j + 1) * 128, :],
                          ob[:, (j % 2) * 128:(j % 2) * 128 + 128])],
                        waits=[m_fin2[j]])
                m_fin_last = m_fin2[7]

    if ATTN_DBG:
        dbg_co = k.dram("dbg_co", [4 * S, 128], BF16, kind="ExternalOutput")
        k.load("sync", [(dbg_co, contrib_o)])
    if ATTN_UPTO < 6:
        return
    k.collective("AllGather", ALU.bypass, [contrib_o], [ago])

    with ExitStack() as stk:
        o_fm = k.sb("atof", [128, 16 * TPC], BF16, stk=stk)
        with ExitStack() as stk2:
            o_tm = k.sb("atot", [128, 16 * 128 * 16], BF16, stk=stk2)
            ptr = k.psum("atpt", [128, 512], BF16, stk=stk2)
            otv = o_tm[:].rearrange("p (tt v) -> p tt v", tt=16)

            oc = k.dram(k.uid("at_oc"), [16 * TPC, 128], BF16)

            def ocanon(gp):
                pid = nc.partition_id([gp.engine])
                agov = ago.rearrange("(r bb e q) w -> r bb e (q w)", r=8, bb=2, e=2)
                odst = oc.rearrange("(r e q) w -> r e (q w)", r=8, e=2)
                for e in range(2):
                    src = agov[:, bass.ds(pid // 4, 1), e,
                               bass.ds((pid % 4) * (2048 * 128), 2048 * 128)]
                    gp.dma_start(out=odst[:, e],
                                 in_=src.squeeze(axis=1)).then_inc(k.gld.h, 16)
                    k.gld.n += 16
            k.step("gpsimd", ocanon)

            k.load("sync", [(otv[:, :, h * 128:(h + 1) * 128],
                             oc[h * TPC:(h + 1) * TPC, :].rearrange("(tt p) v -> p tt v", p=128))
                            for h in range(16)])

            for vt_i in range(16):
                for tq in range(4):
                    def tr(t, vt_i=vt_i, tq=tq):
                        inst = None
                        for u in range(4):
                            tt = tq * 4 + u
                            inst = t.transpose(ptr[:, u * 128:(u + 1) * 128],
                                               otv[:, tt, vt_i * 128:(vt_i + 1) * 128],
                                               i128bf)
                        return inst
                    k.step("tensor", tr)

                    def ev(v, vt_i=vt_i, tq=tq):
                        return v.tensor_copy(
                            o_fm[:, vt_i * TPC + tq * 512:vt_i * TPC + (tq + 1) * 512],
                            ptr[:])
                    k.step("vector", ev)

        xg = k.sb("atxg2", [128, ND * GT], F32, stk=stk)
        xnew = k.sb("atxn", [128, 2 * GT], F32, stk=stk)
        wob = k.sb("atwo", [128, 16 * 1024], BF16, stk=stk)
        pA = k.psum("atpA2", [128, 1024], stk=stk)
        pB = k.psum("atpB2", [128, 1024], stk=stk)
        pO = k.psum("atpO2", [128, 2048], stk=stk)
        wov = woutt.rearrange("(vt p) o -> p vt o", p=128)

        for g in range(NG):
            gsl = slice(g * GT, (g + 1) * GT)
            k.load("sync", [(xg[:, dt * GT:(dt + 1) * GT],
                             xin[dt * 128:(dt + 1) * 128, gsl]) for dt in range(ND)])
            for rnd in range(2):
                osl = slice(rnd * 1024, (rnd + 1) * 1024)
                k.load("gpsimd", [(wob[:].rearrange("p (vt o) -> p vt o", vt=16),
                                   wov[:, :, osl])])
                banks = [pO[:, 0:512], pO[:, 512:1024], pO[:, 1024:1536],
                         pO[:, 1536:2048], pA[:, 0:512], pA[:, 512:1024],
                         pB[:, 0:512], pB[:, 512:1024]]

                def mmo(t, g=g, banks=banks):
                    inst = None
                    for vt_i in range(16):
                        for dtt in range(8):
                            inst = t.matmul(banks[dtt],
                                            wob[:, vt_i * 1024 + dtt * 128:vt_i * 1024 + (dtt + 1) * 128],
                                            o_fm[:, vt_i * TPC + g * GT:vt_i * TPC + (g + 1) * GT],
                                            start=(vt_i == 0), stop=(vt_i == 15))
                    return inst
                k.step("tensor", mmo)

                for dtt in range(8):
                    dt = rnd * 8 + dtt
                    buf = dtt % 2

                    def epi(v, dtt=dtt, dt=dt, buf=buf, banks=banks):
                        return v.scalar_tensor_tensor(
                            xnew[:, buf * GT:(buf + 1) * GT], banks[dtt], 1.0,
                            xg[:, dt * GT:(dt + 1) * GT], ALU.mult, ALU.add)
                    k.step("vector", epi)
                    k.load("sync", [(xout[dt * 128:(dt + 1) * 128, gsl],
                                     xnew[:, buf * GT:(buf + 1) * GT])])


def emit_outproj(k: KB, hc_dram, hccols, xin, xout, woutt, i128bf, gate_dram=None):
    nhb = D // hccols
    with ExitStack() as stk:
        o_fm = k.sb("opof", [128, 16 * TPC], BF16, stk=stk)
        m_ev_last = ZMARK
        with ExitStack() as stk2:
            o_tm = k.sb("opot", [128, 16 * D], BF16, stk=stk2)
            ptr1 = k.psum("oppt1", [128, 512], BF16, stk=stk2)
            ptr2 = k.psum("oppt2", [128, 512], BF16, stk=stk2)
            otv = o_tm[:].rearrange("p (tt v) -> p tt v", tt=16)

            k.load("sync", [(otv[:, :, h * hccols:(h + 1) * hccols],
                             hc_dram[h * TPC:(h + 1) * TPC, :].rearrange(
                                 "(tt p) v -> p tt v", p=128))
                            for h in range(nhb)])
            m_gate = ZMARK
            if gate_dram is not None:
                gt = k.sb("opgt", [128, 2 * D], BF16, stk=stk2)
                gsrc = gate_dram.rearrange("(tt p) v -> tt p v", p=128)
                m_gmul = {}
                gl_after = {}
                for tt in range(16):
                    buf = tt % 2
                    gl_after[tt] = k.load(
                        "sync", [(gt[:, buf * D:(buf + 1) * D], gsrc[tt])],
                        waits="all" if tt == 0 else [m_gmul.get(tt - 2, ZMARK)])

                    def gmul(v, tt=tt, buf=buf):
                        return v.tensor_mul(otv[:, tt, :], otv[:, tt, :],
                                            gt[:, buf * D:(buf + 1) * D])
                    m_gmul[tt] = k.step("vector", gmul,
                                        waits=[(k.sld, gl_after[tt])])
                m_gate = m_gmul[15]
            else:
                k.barrier()

            m_tr = {}
            m_evc = {}
            i = 0
            for vt_i in range(16):
                for tq in range(4):
                    par = i % 2
                    ptr = ptr1 if par == 0 else ptr2

                    def tr(t, vt_i=vt_i, tq=tq, ptr=ptr):
                        inst = None
                        for u in range(4):
                            tt = tq * 4 + u
                            inst = t.transpose(ptr[:, u * 128:(u + 1) * 128],
                                               otv[:, tt, vt_i * 128:(vt_i + 1) * 128],
                                               i128bf)
                        return inst
                    m_tr[i] = k.step("tensor", tr,
                                     waits=[m_gate, m_evc.get(i - 2, ZMARK)])

                    def ev(v, vt_i=vt_i, tq=tq, ptr=ptr):
                        return v.tensor_copy(
                            o_fm[:, vt_i * TPC + tq * 512:vt_i * TPC + (tq + 1) * 512],
                            ptr[:])
                    m_evc[i] = k.step("vector", ev, waits=[m_tr[i]])
                    i += 1
            m_ev_last = m_evc[i - 1]

        xg = k.sb("opxg", [128, ND * GT], F32, stk=stk)
        xnew = k.sb("opxn", [128, 8 * GT], F32, stk=stk)
        wob = k.sb("opwo", [128, 16 * 1024], BF16, stk=stk)
        pA = k.psum("oppA", [128, 1024], stk=stk)
        pB = k.psum("oppB", [128, 1024], stk=stk)
        pO = k.psum("oppO", [128, 2048], stk=stk)
        wov = woutt.rearrange("(vt p) o -> p vt o", p=128)
        m_epi = ZMARK
        m_mmq = {}
        st_after = 0
        xl_after = 0

        for g in range(NG):
            gsl = slice(g * GT, (g + 1) * GT)
            xl_after = k.load(
                "sync",
                [(xg[:].rearrange("p (dt t) -> p dt t", dt=ND),
                  xin[:, gsl].rearrange("(dt p) t -> p dt t", p=128))],
                waits=[m_epi] if g else "all")
            for rnd in range(2):
                osl = slice(rnd * 1024, (rnd + 1) * 1024)
                banks = [pO[:, 0:512], pO[:, 512:1024], pO[:, 1024:1536],
                         pO[:, 1536:2048], pA[:, 0:512], pA[:, 512:1024],
                         pB[:, 0:512], pB[:, 512:1024]]
                for q in range(4):
                    wlq = k.load("gpsimd",
                                 [(wob[:, q * 4096:(q + 1) * 4096].rearrange(
                                     "p (vt o) -> p vt o", vt=4),
                                   wov[:, 4 * q:4 * q + 4, osl])],
                                 waits=[m_mmq.get(q, ZMARK), m_ev_last])

                    def mmo(t, q=q, banks=banks):
                        inst = None
                        for vv in range(4):
                            vt_i = 4 * q + vv
                            for dtt in range(8):
                                inst = t.matmul(banks[dtt],
                                                wob[:, q * 4096 + vv * 1024 + dtt * 128:
                                                    q * 4096 + vv * 1024 + (dtt + 1) * 128],
                                                o_fm[:, vt_i * TPC + g * GT:vt_i * TPC + (g + 1) * GT],
                                                start=(vt_i == 0), stop=(vt_i == 15))
                            vt_i += 1
                        return inst
                    m_mmq[q] = k.step("tensor", mmo,
                                      waits=[(k.gld, wlq), m_ev_last, m_epi,
                                             (k.sld, xl_after)])

                em = ZMARK
                for dtt in range(8):
                    dt = rnd * 8 + dtt

                    def epi(v, dtt=dtt, dt=dt, banks=banks):
                        return v.scalar_tensor_tensor(
                            xnew[:, dtt * GT:(dtt + 1) * GT], banks[dtt], 1.0,
                            xg[:, dt * GT:(dt + 1) * GT], ALU.mult, ALU.add)
                    em = k.step("vector", epi,
                                waits=[m_mmq[3], (k.sld, st_after)])
                m_epi = em
                st_after = k.load(
                    "sync",
                    [(xout[rnd * 1024:(rnd + 1) * 1024, gsl].rearrange("(dtt p) t -> p dtt t", p=128),
                      xnew[:].rearrange("p (dtt t) -> p dtt t", dtt=8))],
                    waits=[m_epi])
    stk0_close = None


EPS_ML = 1e-6
LNS128 = float(np.log(1.0 / np.sqrt(128.0)))


def emit_mlstm(k: KB, ln: LnEnv, xin, xout, wqkT, wvT, wogT, wifT, ifbias, gmt_in,
               woutt, i128bf, i128f, trimp):
    nc = k.nc
    contrib = k.dram("ml_contrib", [4096, TPC], BF16)
    ag = k.dram("ml_ag", [NCORE * 4096, TPC], BF16, addr_space="Shared")
    contrib_if = k.dram("ml_cif", [16, TPC], F32)
    agif = k.dram("ml_agif", [NCORE * 16, TPC], F32, addr_space="Shared")
    contrib_h = k.dram("ml_ch", [2 * S, 256], BF16)
    agh = k.dram("ml_agh", [NCORE * 2 * S, 256], BF16, addr_space="Shared")
    og_l = k.dram("ml_og", [TPC, D], BF16)

    with ExitStack() as stk:
        xg = k.sb("mlxg", [128, ND * GT], F32, stk=stk)
        xlng = k.sb("mlxl", [128, ND * GT], BF16, stk=stk)
        wblk = k.sb("mlwb", [128, 2 * ND * 512], BF16, stk=stk)
        wifb = k.sb("mlwi", [128, ND * 16], BF16, stk=stk)
        ifbt = k.sb("mlib", [128, 16], F32, stk=stk)
        s15 = k.sb("mls15", [128, 1], F32, stk=stk)
        qko = k.sb("mlqk", [128, 2 * 512], BF16, stk=stk)
        sc1 = k.sb("mlsc", [128, 16], F32, stk=stk)
        ifr = k.sb("mlifr", [16, 128], F32, stk=stk)
        pA = k.psum("mlpA", [128, 1024], stk=stk)
        pB = k.psum("mlpB", [128, 1024], stk=stk)
        pT = k.psum("mlpT", [128, 512], stk=stk)
        wqv = wqkT.rearrange("(dt p) o -> p dt o", p=128)
        wvv = wvT.rearrange("(dt p) o -> p dt o", p=128)
        wov = wogT.rearrange("(dt p) o -> p dt o", p=128)
        k.barrier()

        k.load("sync", [(ifbt[:], ifbias)])

        def s15i(v):
            return v.memset(s15[:], 1.0 / 15.0)
        k.step("vector", s15i)
        k.load("gpsimd", [(wifb[:].rearrange("p (dt o) -> p dt o", dt=ND),
                           wifT.rearrange("(dt p) o -> p dt o", p=128))])
        m_mm = {}
        m_evc = {}
        st_q = {}

        for g in range(NG):
            gsl = slice(g * GT, (g + 1) * GT)
            k.load("sync", [(xg[:].rearrange("p (dt t) -> p dt t", dt=ND),
                             xin[:, gsl].rearrange("(dt p) t -> p dt t", p=128))])
            mark_apply = emit_ln(k, ln, xg, xlng, pA, pB)

            for ot in range(16):
                buf = ot % 2
                osl = slice(ot * 128, (ot + 1) * 128)
                wl = k.load("gpsimd",
                            [(wblk[:, buf * 2048:buf * 2048 + 2048].rearrange(
                                "p (dt h) -> p dt h", dt=ND), wqv[:, :, osl])],
                            waits=[m_mm.get(("qk", g, ot - 2),
                                            m_mm.get(("w", 1, g - 1, 2, 3), ZMARK))])
                bank = pA[:, 0:512] if buf == 0 else pA[:, 512:1024]

                def mmqk(t, buf=buf, bank=bank):
                    inst = None
                    for dt in range(ND):
                        inst = t.matmul(bank, wblk[:, buf * 2048 + dt * 128:buf * 2048 + (dt + 1) * 128],
                                        xlng[:, dt * GT:(dt + 1) * GT],
                                        start=(dt == 0), stop=(dt == ND - 1))
                    return inst
                m_mm[("qk", g, ot)] = k.step(
                    "tensor", mmqk,
                    waits=[(k.gld, wl), mark_apply,
                           m_evc.get(("qk", g, ot - 2), m_evc.get(("last", g - 1), ZMARK))])

                def ev(v, buf=buf, bank=bank):
                    return v.tensor_copy(qko[:, buf * 512:(buf + 1) * 512], bank)
                m_evc[("qk", g, ot)] = k.step(
                    "vector", ev,
                    waits=[m_mm[("qk", g, ot)],
                           (k.sld, st_q.get(("qk", g, ot - 2),
                                            st_q.get(("w", 1, g - 1, 3, 2 + (ot % 2)), 0)))])
                st_q[("qk", g, ot)] = k.load(
                    "sync", [(contrib[ot * 128:(ot + 1) * 128, gsl],
                              qko[:, buf * 512:(buf + 1) * 512])],
                    waits=[m_evc[("qk", g, ot)]])

            prev_ev = m_evc[("qk", g, 15)]
            for which, wsrc in ((0, wvv), (1, wov)):
                for vb in range(4):
                    buf = vb % 2
                    vsl = slice(vb * 512, (vb + 1) * 512)
                    if which == 0 and vb == 0:
                        wfall = m_mm[("qk", g, 15)]
                    elif which == 0 and vb == 1:
                        wfall = m_mm.get(("w", 1, g - 1, 3, 3), ZMARK)
                    elif which == 1 and vb < 2:
                        wfall = m_mm[("w", 0, g, 2 + vb, 3)]
                    else:
                        wfall = ZMARK
                    wl = k.load("gpsimd",
                                [(wblk[:, buf * 8192:(buf + 1) * 8192].rearrange(
                                    "p (dt h) -> p dt h", dt=ND), wsrc[:, :, vsl])],
                                waits=[m_mm.get(("w", which, g, vb - 2, 3), wfall)])
                    for ttl in range(4):
                        pbuf = ttl % 2
                        bank = pB[:, 0:512] if pbuf == 0 else pB[:, 512:1024]

                        def mmv(t, ttl=ttl, buf=buf, bank=bank):
                            inst = None
                            for dt in range(ND):
                                inst = t.matmul(bank,
                                                xlng[:, dt * GT + ttl * 128:dt * GT + (ttl + 1) * 128],
                                                wblk[:, buf * 8192 + dt * 512:buf * 8192 + (dt + 1) * 512],
                                                start=(dt == 0), stop=(dt == ND - 1))
                            return inst
                        m_mm[("w", which, g, vb, ttl)] = k.step(
                            "tensor", mmv,
                            waits=[(k.gld, wl), mark_apply,
                                   m_evc.get(("w", which, g, vb, ttl - 2),
                                             m_evc.get(("w", which, g, vb - 1, 2 + ttl),
                                                       prev_ev))])
                        qbuf = (vb * 4 + ttl) % 2

                        if which == 0:
                            def ev2(v, qbuf=qbuf, bank=bank):
                                return v.tensor_copy(qko[:, qbuf * 512:(qbuf + 1) * 512], bank)
                            m_evc[("w", which, g, vb, ttl)] = k.step(
                                "vector", ev2,
                                waits=[m_mm[("w", which, g, vb, ttl)],
                                       (k.sld, st_q.get(("w", which, g, vb, ttl - 2),
                                                        st_q.get(("w", which, g, vb - 1, 2 + ttl),
                                                                 st_q.get(("qk", g, 14 + qbuf), 0))))])
                            dst = contrib[2048 + g * GT + ttl * 128:
                                          2048 + g * GT + (ttl + 1) * 128,
                                          vb * 512:(vb + 1) * 512]
                        else:
                            def ev2(s, qbuf=qbuf, bank=bank):
                                return s.activation(qko[:, qbuf * 512:(qbuf + 1) * 512],
                                                    bank, AF.Sigmoid)
                            m_evc[("w", which, g, vb, ttl)] = k.step(
                                "scalar", ev2,
                                waits=[m_mm[("w", which, g, vb, ttl)],
                                       (k.sld, st_q.get(("w", which, g, vb, ttl - 2),
                                                        st_q.get(("w", which, g, vb - 1, 2 + ttl),
                                                                 st_q.get(("w", 0, g, 3, 2 + ttl), 0))))])
                            dst = og_l[g * GT + ttl * 128:g * GT + (ttl + 1) * 128,
                                       vb * 512:(vb + 1) * 512]
                        st_q[("w", which, g, vb, ttl)] = k.load(
                            "sync", [(dst, qko[:, qbuf * 512:(qbuf + 1) * 512])],
                            waits=[m_evc[("w", which, g, vb, ttl)]])
                prev_ev = m_evc[("w", which, g, 3, 3)]

            for ttl in range(4):
                def mmif(t, ttl=ttl):
                    inst = None
                    for dt in range(ND):
                        inst = t.matmul(pT[:, 0:16],
                                        xlng[:, dt * GT + ttl * 128:dt * GT + (ttl + 1) * 128],
                                        wifb[:, dt * 16:(dt + 1) * 16],
                                        start=(dt == 0), stop=(dt == ND - 1))
                    return inst
                k.step("tensor", mmif)

                def addb(v):
                    return v.tensor_add(sc1[:], pT[:, 0:16], ifbt[:])
                k.step("vector", addb)

                def cap1(s):
                    return s.activation(sc1[:], sc1[:], AF.Tanh, scale=s15[:])
                k.step("scalar", cap1)

                def cap2(v):
                    return v.tensor_scalar_mul(sc1[:], sc1[:], 15.0)
                k.step("vector", cap2)

                def trif(t):
                    return t.transpose(pT[0:16, 256:384], sc1[:], i128f)
                k.step("tensor", trif)

                def evif(v):
                    return v.tensor_copy(ifr[:], pT[0:16, 256:384])
                k.step("vector", evif)
                k.load("sync", [(contrib_if[:, g * GT + ttl * 128:g * GT + (ttl + 1) * 128],
                                 ifr[:])])
            m_evc[("last", g)] = m_evc[("w", 1, g, 3, 3)]

    k.collective("AllGather", ALU.bypass, [contrib], [ag])
    k.collective("AllGather", ALU.bypass, [contrib_if], [agif])

    qcm = k.dram("ml_qc", [1024, 2048], BF16)
    kcm = k.dram("ml_kc", [1024, 2048], BF16)
    vcm = k.dram("ml_vc", [TOK, 256], BF16)
    ifcm = k.dram("ml_ifc", [4, S], F32)

    def mcanon(sy):
        pid = nc.partition_id([sy.engine])
        agv = ag.rearrange("(b rl half sec hh p) t -> b rl half sec hh p t",
                           b=2, rl=4, half=2, sec=2, hh=8)
        qdst = qcm.rearrange("(b rl p) t -> b rl p t", b=2, rl=4)
        kdst = kcm.rearrange("(b rl p) t -> b rl p t", b=2, rl=4)
        sy.dma_start(out=qdst, in_=agv[:, :, 0, 0, bass.ds(pid, 1)].squeeze(axis=2)
                     ).then_inc(k.sld.h, 16)
        k.sld.n += 16
        sy.dma_start(out=kdst, in_=agv[:, :, 0, 1, bass.ds(pid, 1)].squeeze(axis=2)
                     ).then_inc(k.sld.h, 16)
        k.sld.n += 16
        agw = ag.rearrange("(b rl half tok) (hv cv) -> b rl half tok hv cv",
                           b=2, rl=4, half=2, hv=8)
        vdst = vcm.rearrange("(b rl tok) c -> b rl tok c", b=2, rl=4)
        for bb in range(2):
            for rl in range(4):
                sy.dma_start(out=vdst[bb, rl],
                             in_=agw[bb, rl, 1, :, bass.ds(pid, 1), :].squeeze(axis=1)
                             ).then_inc(k.sld.h, 16)
                k.sld.n += 16
        agi = agif.rearrange("(b rl w x) t -> b rl w x t", b=2, rl=4, w=2)
        ifdst = ifcm.rearrange("(b x) (rl t) -> b rl x t", b=2, rl=4)
        for x in range(2):
            sy.dma_start(out=ifdst[:, :, x],
                         in_=agi[:, :, x, bass.ds(pid, 1)].squeeze(axis=2)
                         ).then_inc(k.sld.h, 16)
            k.sld.n += 16
    k.step("sync", mcanon)

    NCH = S // 128
    with ExitStack() as stk:
        Qf = k.sb("mlQ", [128, S], BF16, stk=stk)
        Kf = k.sb("mlK", [128, S], BF16, stk=stk)
        vt = k.sb("mlV", [128, 64 * 257], BF16, stk=stk)
        irow = k.sb("mli", [1, S], F32, stk=stk)
        frow = k.sb("mlf", [1, S], F32, stk=stk)
        ones1 = k.sb("mlo1", [1, 128], F32, stk=stk)
        neg1 = k.sb("mln1", [1, 1], F32, stk=stk)
        one1 = k.sb("mlp1", [1, 1], F32, stk=stk)
        negc = k.sb("mlnc", [128, 1], F32, stk=stk)
        erow = k.sb("mler", [1, 128], F32, stk=stk)
        lrow = k.sb("mllr", [1, 128], F32, stk=stk)
        bneg = k.sb("mlbn", [1, 128], F32, stk=stk)
        grow = k.sb("mlgr", [1, 128], F32, stk=stk)
        mrow = k.sb("mlmr", [1, 128], F32, stk=stk)
        mrow2 = k.sb("mlm2", [1, 128], F32, stk=stk)
        mmrow = k.sb("mlmm", [1, 128], F32, stk=stk)
        rowsb = k.sb("mlrb", [1, 256], BF16, stk=stk)
        gsbl = k.sb("mlgl", [128, 1], F32, stk=stk)
        gsbr = k.sb("mlgw", [128, 1], F32, stk=stk)
        msb = k.sb("mlms", [128, 1], F32, stk=stk)
        m2sb = k.sb("mlm2s", [128, 1], F32, stk=stk)
        emc = k.sb("mlem", [128, 1], F32, stk=stk)
        aq = k.sb("mlaq", [128, 1], F32, stk=stk)
        den = k.sb("mlden", [128, 1], F32, stk=stk)
        rden = k.sb("mlrd", [128, 1], F32, stk=stk)
        mcol = k.sb("mlmc", [128, 1], F32, stk=stk)
        mlns = k.sb("mlml", [128, 1], F32, stk=stk)
        ccol = k.sb("mlcc", [128, 1], F32, stk=stk)
        phc = k.sb("mlph", [128, 1], F32, stk=stk)
        wc = k.sb("mlwc", [128, 1], F32, stk=stk)
        bnl = k.sb("mlbl", [128, 1], F32, stk=stk)
        Dt = k.sb("mlDt", [128, 128], F32, stk=stk)
        ptb = k.sb("mlpt", [128, 128], BF16, stk=stk)
        kw = k.sb("mlkw", [128, 128], BF16, stk=stk)
        CnA = k.sb("mlCa", [128, 257], F32, stk=stk)
        CnB = k.sb("mlCb", [128, 257], F32, stk=stk)
        Cnb = k.sb("mlCc", [128, 257], BF16, stk=stk)
        hIs = k.sb("mlhi", [128, 257], F32, stk=stk)
        hps = k.sb("mlhp", [128, 257], F32, stk=stk)
        hsc = k.sb("mlhs", [128, 256], F32, stk=stk)
        hsq = k.sb("mlhq", [128, 256], F32, stk=stk)
        srn = k.sb("mlsr", [128, 4], F32, stk=stk)
        hnn = k.sb("mlhn", [128, 256], BF16, stk=stk)
        gmt = k.sb("mlgm", [128, 256], F32, stk=stk)
        pMB = k.psum("mlMB", [128, 256], stk=stk)
        pS = k.psum("mlpS", [128, 128], stk=stk)
        pI = k.psum("mlpI", [128, 512], stk=stk)
        pPT = k.psum("mlpP", [128, 512], stk=stk)
        pD = k.psum("mlpD", [128, 512], stk=stk)
        pG1 = k.psum("mlg1", [128, 2], stk=stk)
        pG2 = k.psum("mlg2", [128, 2], stk=stk)
        pKt = k.psum("mlKt", [128, 128], BF16, stk=stk)
        vtv = vt[:].rearrange("p (c w) -> p c w", w=257)

        def initc(v):
            v.memset(ones1[:], 1.0)
            v.memset(neg1[:], -1.0)
            v.memset(one1[:], 1.0)
            v.memset(negc[:], -1.0)
            return v.memset(vtv[:, :, 256:257], 1.0)
        k.step("vector", initc)
        k.load("sync", [(gmt[:], gmt_in)])

        for slot in range(2):
            qk = [(Qf[:, rl * 2048:(rl + 1) * 2048],
                   qcm[slot * 512 + rl * 128:slot * 512 + (rl + 1) * 128, :])
                  for rl in range(4)]
            qk += [(Kf[:, rl * 2048:(rl + 1) * 2048],
                    kcm[slot * 512 + rl * 128:slot * 512 + (rl + 1) * 128, :])
                   for rl in range(4)]
            qk.append((vtv[:, :, 0:256],
                       vcm[slot * 8192:(slot + 1) * 8192, :].rearrange("(c p) w -> p c w", p=128)))
            qk.append((irow[:], ifcm[2 * slot:2 * slot + 1, :]))
            qk.append((frow[:], ifcm[2 * slot + 1:2 * slot + 2, :]))
            k.load("sync", qk)

            def zero_state(v):
                v.memset(CnA[:], 0.0)
                v.memset(Cnb[:], 0.0)
                return v.memset(mcol[:], 0.0)
            k.step("vector", zero_state)

            Cn_cur, Cn_nxt = CnA, CnB
            for c in range(NCH):
                csl = slice(c * 128, (c + 1) * 128)

                def e1(s, csl=csl):
                    return s.activation(erow[:], frow[0:1, csl], AF.Exp, scale=neg1[:])
                k.step("scalar", e1)

                def l1(s):
                    return s.activation(lrow[:], erow[:], AF.Ln, bias=one1[:])
                k.step("scalar", l1)

                def cums(v):
                    return v.tensor_tensor_scan(bneg[:], ones1[:], lrow[:], 0.0,
                                                ALU.mult, ALU.add)
                k.step("vector", cums)

                def gr(v, csl=csl):
                    return v.tensor_add(grow[:], irow[0:1, csl], bneg[:])
                k.step("vector", gr)

                def cumm(v):
                    return v.tensor_tensor_scan(mrow[:], ones1[:], grow[:], -1e30,
                                                ALU.mult, ALU.max)
                k.step("vector", cumm)

                def mmax(v):
                    return v.tensor_scalar_max(mrow2[:], mrow[:], mcol[0:1, 0:1])
                k.step("vector", mmax)

                def mmr(v):
                    return v.tensor_sub(mmrow[:], mrow2[:], bneg[:])
                k.step("vector", mmr)

                def rcast(v):
                    v.tensor_copy(rowsb[0:1, 0:128], mrow2[:])
                    return v.tensor_copy(rowsb[0:1, 128:256], bneg[:])
                k.step("vector", rcast)

                def bc(t):
                    t.matmul(pMB[:, 0:128], ln.onesrow[:], rowsb[0:1, 0:128],
                             start=True, stop=False)
                    t.matmul(pMB[:, 0:128], i128bf, trimp,
                             start=False, stop=False, skip_group_check=True)
                    return t.matmul(pMB[:, 128:256], ln.onesrow[:], rowsb[0:1, 128:256],
                                    start=False, stop=True, skip_group_check=True)
                k.step("tensor", bc)

                def trgm(t):
                    t.transpose(pG1[:, 0:1], grow[:], i128f[0:1, 0:1])
                    return t.transpose(pG2[:, 0:1], mmrow[:], i128f[0:1, 0:1])
                k.step("tensor", trgm)

                def gcols(v):
                    v.tensor_scalar_add(gsbl[:], pG1[:, 0:1], LNS128)
                    v.tensor_copy(gsbr[:], pG1[:, 0:1])
                    v.tensor_copy(msb[:], pG2[:, 0:1])
                    return v.tensor_scalar_add(mlns[:], mcol[:], LNS128)
                k.step("vector", gcols)

                def trm2(t):
                    return t.transpose(pG1[:, 0:1], mrow2[:], i128f[0:1, 0:1])
                k.step("tensor", trm2)

                def m2c(v):
                    return v.tensor_copy(m2sb[:], pG1[:, 0:1])
                k.step("vector", m2c)

                def smm(t, csl=csl):
                    return t.matmul(pS[:], Kf[:, csl], Qf[:, csl], start=True, stop=True)
                k.step("tensor", smm)

                def dexp(s):
                    return s.activation(Dt[:], pMB[:, 0:128], AF.Exp,
                                        scale=negc[:], bias=gsbl[:])
                k.step("scalar", dexp)

                def pmul(v):
                    return v.tensor_mul(ptb[:], pS[:], Dt[:])
                k.step("vector", pmul)

                def immm(t, csl=csl):
                    return t.matmul(pI[:, 0:257], Qf[:, csl], Cnb[:], start=True, stop=True)
                k.step("tensor", immm)

                def tmm(t, c=c):
                    return t.matmul(pPT[:, 0:257], ptb[:], vtv[:, c, :], start=True, stop=True)
                k.step("tensor", tmm)

                def cc2(s):
                    return s.activation(ccol[:], m2sb[:], AF.Exp,
                                        scale=negc[:], bias=mlns[:])
                k.step("scalar", cc2)

                def hi(s):
                    return s.activation(hIs[:], pI[:, 0:257], AF.Copy, scale=ccol[:])
                k.step("scalar", hi)

                def hp(v):
                    return v.tensor_add(hps[:], hIs[:], pPT[:, 0:257])
                k.step("vector", hp)

                def em(s):
                    return s.activation(emc[:], msb[:], AF.Exp, scale=negc[:])
                k.step("scalar", em)

                def aqs(s):
                    return s.activation(aq[:], hps[:, 256:257], AF.Abs)
                k.step("scalar", aqs)

                def dn(v):
                    return v.tensor_max(den[:], aq[:], emc[:])
                k.step("vector", dn)

                def dn2(v):
                    return v.tensor_scalar_add(den[:], den[:], EPS_ML)
                k.step("vector", dn2)

                def rd(v):
                    return v.reciprocal(rden[:], den[:])
                k.step("vector", rd)

                def hh(s):
                    return s.activation(hsc[:], hps[:, 0:256], AF.Copy, scale=rden[:])
                k.step("scalar", hh)

                def n1(v):
                    return v.reduce_sum(srn[:, 0:1], hsc[:], mybir.AxisListType.X)
                k.step("vector", n1)

                def n2(s):
                    return s.activation(hsq[:], hsc[:], AF.Square)
                k.step("scalar", n2)

                def n3(v):
                    return v.reduce_sum(srn[:, 1:2], hsq[:], mybir.AxisListType.X)
                k.step("vector", n3)

                def n4(v):
                    v.tensor_scalar_mul(srn[:, 0:1], srn[:, 0:1], 1.0 / 256.0)
                    return v.tensor_scalar_mul(srn[:, 1:2], srn[:, 1:2], 1.0 / 256.0)
                k.step("vector", n4)

                def n5(v):
                    return v.tensor_mul(srn[:, 2:3], srn[:, 0:1], srn[:, 0:1])
                k.step("vector", n5)

                def n6(v):
                    return v.tensor_sub(srn[:, 2:3], srn[:, 1:2], srn[:, 2:3])
                k.step("vector", n6)

                def n7(v):
                    return v.tensor_scalar_add(srn[:, 2:3], srn[:, 2:3], EPS_ML)
                k.step("vector", n7)

                def n8(s):
                    return s.activation(srn[:, 3:4], srn[:, 2:3], AF.Sqrt)
                k.step("scalar", n8)

                def n9(v):
                    return v.reciprocal(srn[:, 2:3], srn[:, 3:4])
                k.step("vector", n9)

                def n10(v):
                    return v.tensor_scalar(hsq[:], hsc[:], srn[:, 0:1], srn[:, 2:3],
                                           ALU.subtract, ALU.mult)
                k.step("vector", n10)

                def n11(v):
                    return v.tensor_mul(hnn[:], hsq[:], gmt[:])
                k.step("vector", n11)
                k.load("sync", [(contrib_h[slot * S + c * 128:slot * S + (c + 1) * 128, :],
                                 hnn[:])])

                def wcs(s):
                    return s.activation(wc[:], pMB[:, 127:128], AF.Exp,
                                        scale=negc[:], bias=gsbr[:])
                k.step("scalar", wcs)

                def ktr(t, csl=csl):
                    return t.transpose(pKt[:], Kf[:, csl], i128bf)
                k.step("tensor", ktr)

                def kws(v):
                    return v.tensor_scalar_mul(kw[:], pKt[:], wc[:])
                k.step("vector", kws)

                def dmm(t, c=c):
                    return t.matmul(pD[:, 0:257], kw[:], vtv[:, c, :], start=True, stop=True)
                k.step("tensor", dmm)

                def phs(s):
                    return s.activation(phc[:], pMB[:, 127:128], AF.Exp,
                                        scale=negc[:], bias=mcol[:])
                k.step("scalar", phs)

                def cnew(v, Cn_cur=Cn_cur, Cn_nxt=Cn_nxt):
                    return v.scalar_tensor_tensor(Cn_nxt[:], Cn_cur[:], phc[:],
                                                  pD[:, 0:257], ALU.mult, ALU.add)
                k.step("vector", cnew)

                def ccast(v, Cn_nxt=Cn_nxt):
                    v.tensor_copy(Cnb[:], Cn_nxt[:])
                    return v.tensor_copy(bnl[:], pMB[:, 255:256])
                k.step("vector", ccast)

                def mup(v):
                    return v.tensor_sub(mcol[:], pMB[:, 127:128], bnl[:])
                k.step("vector", mup)

                Cn_cur, Cn_nxt = Cn_nxt, Cn_cur

    if ML_DBG:
        dbg_ifc = k.dram("dbg_ifc", [4, S], F32, kind="ExternalOutput")
        dbg_ch = k.dram("dbg_ch", [2 * S, 256], BF16, kind="ExternalOutput")
        dbg_qcm = k.dram("dbg_qcm", [1024, 2048], BF16, kind="ExternalOutput")
        dbg_kcm = k.dram("dbg_kcm", [1024, 2048], BF16, kind="ExternalOutput")
        dbg_vcm = k.dram("dbg_vcm", [TOK, 256], BF16, kind="ExternalOutput")
        k.load("sync", [(dbg_ifc, ifcm), (dbg_ch, contrib_h), (dbg_qcm, qcm),
                        (dbg_kcm, kcm), (dbg_vcm, vcm)])
    k.collective("AllGather", ALU.bypass, [contrib_h], [agh])

    hcc = k.dram("ml_hcc", [8 * TPC, 256], BF16)

    def hcanon(sy):
        pid = nc.partition_id([sy.engine])
        agv = agh.rearrange("(r b q) w -> r b (q w)", r=8, b=2)
        hdst = hcc.rearrange("(r q) w -> r (q w)", r=8)
        src = agv[:, bass.ds(pid // 4, 1), bass.ds((pid % 4) * (2048 * 256), 2048 * 256)]
        sy.dma_start(out=hdst, in_=src.squeeze(axis=1)).then_inc(k.sld.h, 16)
        k.sld.n += 16
    k.step("sync", hcanon)

    emit_outproj(k, hcc, 256, xin, xout, woutt, i128bf, gate_dram=og_l)


def build_mlstm():
    k = KB()
    xin = k.dram("xin", [D, TPC], F32, kind="ExternalInput")
    wqkT = k.dram("wqkT", [D, 2048], F32, kind="ExternalInput")
    wvT = k.dram("wvT", [D, D], F32, kind="ExternalInput")
    wogT = k.dram("wogT", [D, D], F32, kind="ExternalInput")
    wifT = k.dram("wifT", [D, 16], F32, kind="ExternalInput")
    ifbias = k.dram("ifbias", [128, 16], F32, kind="ExternalInput")
    gmt_in = k.dram("gmt", [128, 256], F32, kind="ExternalInput")
    woutt = k.dram("woutt", [D, D], F32, kind="ExternalInput")
    i128 = k.dram("i128", [128, 128], BF16, kind="ExternalInput")
    i128f_d = k.dram("i128f", [128, 128], F32, kind="ExternalInput")
    trimp_d = k.dram("trimp", [128, 128], BF16, kind="ExternalInput")
    xout = k.dram("xout", [D, TPC], F32, kind="ExternalOutput")
    ln = LnEnv(k)
    i128sb = k.sb("i128s", [128, 128], BF16)
    i128fsb = k.sb("i128fs", [128, 128], F32)
    trisb = k.sb("trps", [128, 128], BF16)
    k.load("sync", [(i128sb[:], i128), (i128fsb[:], i128f_d), (trisb[:], trimp_d)])
    emit_mlstm(k, ln, xin, xout, wqkT, wvT, wogT, wifT, ifbias, gmt_in,
               woutt, i128sb[:], i128fsb[:], trisb[:])
    k.finish()
    return k


def build_attn():
    k = KB()
    xin = k.dram("xin", [D, TPC], F32, kind="ExternalInput")
    wqkvt = k.dram("wqkvt", [D, 3 * D], F32, kind="ExternalInput")
    woutt = k.dram("woutt", [D, D], F32, kind="ExternalInput")
    i128 = k.dram("i128", [128, 128], BF16, kind="ExternalInput")
    trim = k.dram("trim", [128, 128], BF16, kind="ExternalInput")
    xout = k.dram("xout", [D, TPC], F32, kind="ExternalOutput")
    contrib = k.dram("at_contrib", [3 * D, TPC], BF16)
    ag = k.dram("at_ag", [NCORE * 3 * D, TPC], BF16, addr_space="Shared")
    contrib_o = k.dram("at_co", [4 * S, 128], BF16)
    ago = k.dram("at_ago", [NCORE * 4 * S, 128], BF16, addr_space="Shared")
    ln = LnEnv(k)
    i128sb = k.sb("i128s", [128, 128], BF16)
    trisb = k.sb("trims", [128, 128], BF16)
    k.load("sync", [(i128sb[:], i128), (trisb[:], trim)])
    emit_attn(k, ln, xin, xout, wqkvt, woutt, i128sb[:], trisb[:],
              contrib, ag, contrib_o, ago)
    k.finish()
    return k


def build_ffn1():
    k = KB()
    xin = k.dram("xin", [D, TPC], F32, kind="ExternalInput")
    wgt = k.dram("wgt", [D, H1], I32, kind="ExternalInput")
    wut = k.dram("wut", [D, H1], I32, kind="ExternalInput")
    wdt = k.dram("wdt", [H1, D], I32, kind="ExternalInput")
    sg = k.dram("sg", [128, NH1T], F32, kind="ExternalInput")
    su = k.dram("su", [128, NH1T], F32, kind="ExternalInput")
    sd = k.dram("sd", [128, ND], F32, kind="ExternalInput")
    xout = k.dram("xout", [D, TPC], F32, kind="ExternalOutput")
    ln = LnEnv(k)
    emit_ffn1(k, ln, xin, xout, wgt, wut, wdt, sg, su, sd)
    k.finish()
    return k



def _t(a):
    return np.ascontiguousarray(np.asarray(a).T)


def _hadamard(n):
    h = np.array([[1.0]], dtype=np.float32)
    while h.shape[0] < n:
        h = np.block([[h, h], [h, -h]])
    return h


def prep_weights(inp):
    w = {}
    w["w1t"] = _t(inp["L0_ffn_W1"])
    w["w3t"] = _t(inp["L0_ffn_W3"])
    w["w2t"] = _t(inp["L0_ffn_W2"])
    w["h128"] = _hadamard(128).astype(BF)
    w["wgt"] = _t(inp["L1_ffn_wq_gate"])
    w["wut"] = _t(inp["L1_ffn_wq_up"])
    w["wdt"] = _t(inp["L1_ffn_wq_down"])
    w["sg"] = _t(np.asarray(inp["L1_ffn_s_gate"], np.float32).reshape(NH1T, 128))
    w["su"] = _t(np.asarray(inp["L1_ffn_s_up"], np.float32).reshape(NH1T, 128))
    w["sd"] = _t(np.asarray(inp["L1_ffn_s_down"], np.float32).reshape(ND, 128))
    w["wqkvt"] = _t(inp["L1_attn_W_qkv"])
    w["woutt1"] = _t(inp["L1_attn_W_out"])
    w["wqkT"] = np.ascontiguousarray(
        np.concatenate([np.asarray(inp["L0_attn_Wq"]),
                        np.asarray(inp["L0_attn_Wk"])], 0).T)
    w["wvT"] = _t(inp["L0_attn_Wv"])
    w["wogT"] = _t(inp["L0_attn_Wog"])
    w["wifT"] = np.ascontiguousarray(
        np.concatenate([np.asarray(inp["L0_attn_Wig"]),
                        np.asarray(inp["L0_attn_Wfg"])], 0).T)
    w["ifbias"] = np.tile(np.concatenate(
        [np.asarray(inp["L0_attn_big"]),
         np.asarray(inp["L0_attn_bfg"])]).astype(np.float32)[None, :], (128, 1))
    w["woutt0"] = _t(inp["L0_attn_Wout"])
    w["gamma"] = np.asarray(inp["L0_attn_gamma"], np.float32)
    w["i128"] = np.eye(128, dtype=BF)
    w["i128f"] = np.eye(128, dtype=np.float32)
    w["trim"] = np.where(np.arange(128)[:, None] > np.arange(128)[None, :],
                         -30000.0, 0.0).astype(BF)
    w["trimp"] = np.where(np.arange(128)[:, None] > np.arange(128)[None, :],
                          30000.0, 0.0).astype(BF)
    return w


def run_launch(nc, in_maps):
    res = run_bass_kernel_spmd(nc, in_maps, core_ids=list(range(NCORE)))
    return res.results



def build_w2prep():
    k = KB()
    w2t = k.dram("w2t", [H0, D], F32, kind="ExternalInput")
    h128 = k.dram("h128", [128, 128], BF16, kind="ExternalInput")
    out = k.dram("w2ht_part", [H0, DW], BF16, kind="ExternalOutput")
    emit_w2prep(k, w2t, h128, out)
    k.finish()
    return k


def build_ffn0():
    k = KB()
    xin = k.dram("xin", [D, TPC], F32, kind="ExternalInput")
    w1t = k.dram("w1t", [D, H0], F32, kind="ExternalInput")
    w3t = k.dram("w3t", [D, H0], F32, kind="ExternalInput")
    w2ht = k.dram("w2ht", [H0, D], BF16, kind="ExternalInput")
    xout = k.dram("xout", [D, TPC], F32, kind="ExternalOutput")
    ln = LnEnv(k)

    def w2pairs(w2view, sb, rnd):
        return [(w2view.rearrange("p (hh o) -> p hh o", hh=8),
                 w2ht[sb * 1024:(sb + 1) * 1024,
                      rnd * 1024:rnd * 1024 + 1024].rearrange("(hh p) o -> p hh o", p=128))]
    emit_ffn0(k, ln, xin, xout, w1t, w3t, w2pairs)
    k.finish()
    return k


def build_fused():
    k = KB()
    xin = k.dram("xin", [D, TPC], F32, kind="ExternalInput")
    w2t = k.dram("w2t", [H0, D], F32, kind="ExternalInput")
    h128 = k.dram("h128", [128, 128], BF16, kind="ExternalInput")
    w1t = k.dram("w1t", [D, H0], F32, kind="ExternalInput")
    w3t = k.dram("w3t", [D, H0], F32, kind="ExternalInput")
    wqkT = k.dram("wqkT", [D, 2048], F32, kind="ExternalInput")
    wvT = k.dram("wvT", [D, D], F32, kind="ExternalInput")
    wogT = k.dram("wogT", [D, D], F32, kind="ExternalInput")
    wifT = k.dram("wifT", [D, 16], F32, kind="ExternalInput")
    ifbias = k.dram("ifbias", [128, 16], F32, kind="ExternalInput")
    gmt_in = k.dram("gmt", [128, 256], F32, kind="ExternalInput")
    woutt0 = k.dram("woutt0", [D, D], F32, kind="ExternalInput")
    wgt = k.dram("wgt", [D, H1], I32, kind="ExternalInput")
    wut = k.dram("wut", [D, H1], I32, kind="ExternalInput")
    wdt = k.dram("wdt", [H1, D], I32, kind="ExternalInput")
    sg = k.dram("sg", [128, NH1T], F32, kind="ExternalInput")
    su = k.dram("su", [128, NH1T], F32, kind="ExternalInput")
    sd = k.dram("sd", [128, ND], F32, kind="ExternalInput")
    wqkvt = k.dram("wqkvt", [D, 3 * D], F32, kind="ExternalInput")
    woutt1 = k.dram("woutt1", [D, D], F32, kind="ExternalInput")
    i128 = k.dram("i128", [128, 128], BF16, kind="ExternalInput")
    i128f_d = k.dram("i128f", [128, 128], F32, kind="ExternalInput")
    trim_d = k.dram("trim", [128, 128], BF16, kind="ExternalInput")
    trimp_d = k.dram("trimp", [128, 128], BF16, kind="ExternalInput")
    xout = k.dram("xout", [D, TPC], F32, kind="ExternalOutput")

    xa = k.dram("x_a", [D, TPC], F32)
    xb = k.dram("x_b", [D, TPC], F32)
    w2part = k.dram("w2part", [H0, DW], BF16)
    agw2 = k.dram("agw2", [NCORE * H0, DW], BF16, addr_space="Shared")

    emit_w2prep(k, w2t, h128, w2part)
    k.collective("AllGather", ALU.bypass, [w2part], [agw2])

    ln = LnEnv(k)
    i128sb = k.sb("i128s", [128, 128], BF16)
    i128fsb = k.sb("i128fs", [128, 128], F32)
    trisb = k.sb("tris", [128, 128], BF16)
    tripsb = k.sb("trps", [128, 128], BF16)
    k.load("sync", [(i128sb[:], i128), (i128fsb[:], i128f_d),
                    (trisb[:], trim_d), (tripsb[:], trimp_d)])

    def w2pairs(w2view, sb, rnd):
        wv = w2view.rearrange("p (hh qq o) -> p hh qq o", hh=8, qq=4)
        return [(wv[:, :, q, :],
                 agw2[(rnd * 4 + q) * H0 + sb * 1024:(rnd * 4 + q) * H0 + (sb + 1) * 1024,
                      :].rearrange("(hh p) o -> p hh o", p=128))
                for q in range(4)]

    emit_ffn0(k, ln, xin, xa, w1t, w3t, w2pairs)
    emit_mlstm(k, ln, xa, xb, wqkT, wvT, wogT, wifT, ifbias, gmt_in,
               woutt0, i128sb[:], i128fsb[:], tripsb[:])
    emit_ffn0(k, ln, xb, xa, w1t, w3t, w2pairs)
    emit_ffn1(k, ln, xa, xb, wgt, wut, wdt, sg, su, sd)
    emit_attn(k, ln, xb, xa, wqkvt, woutt1, i128sb[:], trisb[:],
              k.dram("at_contrib", [3 * D, TPC], BF16),
              k.dram("at_ag", [NCORE * 3 * D, TPC], BF16, addr_space="Shared"),
              k.dram("at_co", [4 * S, 128], BF16),
              k.dram("at_ago", [NCORE * 4 * S, 128], BF16, addr_space="Shared"))
    emit_ffn1(k, ln, xa, xout, wgt, wut, wdt, sg, su, sd)
    k.finish()
    return k


FUSED = True


def kernel(**inputs):
    w = prep_weights(inputs)
    x0 = np.asarray(inputs["L0_attn_x"], dtype=np.float32).reshape(TOK, D)
    xs = [np.ascontiguousarray(x0[c * TPC:(c + 1) * TPC].T) for c in range(NCORE)]

    if FUSED:
        kf = build_fused()
        in_maps = []
        for c in range(NCORE):
            in_maps.append({
                "xin": xs[c], "w2t": w["w2t"], "h128": w["h128"],
                "w1t": w["w1t"], "w3t": w["w3t"],
                "wqkT": w["wqkT"], "wvT": w["wvT"], "wogT": w["wogT"],
                "wifT": w["wifT"], "ifbias": w["ifbias"],
                "gmt": np.tile(w["gamma"][c][None, :], (128, 1)),
                "woutt0": w["woutt0"],
                "wgt": w["wgt"], "wut": w["wut"], "wdt": w["wdt"],
                "sg": w["sg"], "su": w["su"], "sd": w["sd"],
                "wqkvt": w["wqkvt"], "woutt1": w["woutt1"],
                "i128": w["i128"], "i128f": w["i128f"],
                "trim": w["trim"], "trimp": w["trimp"],
            })
        res = run_launch(kf.nc, in_maps)
        out = np.concatenate([res[c]["xout"].T for c in range(NCORE)], axis=0)
        return np.ascontiguousarray(out.reshape(B, S, D).astype(np.float32))

    k0 = build_w2prep()
    res0 = run_launch(k0.nc, [{"w2t": w["w2t"], "h128": w["h128"]}] * NCORE)
    w2ht = np.concatenate([res0[c]["w2ht_part"] for c in range(NCORE)], axis=1)

    kf0 = build_ffn0()
    km = build_mlstm()
    kf1 = build_ffn1()
    ka = build_attn()

    def ffn0(xs):
        res = run_launch(kf0.nc, [{"xin": xs[c], "w1t": w["w1t"], "w3t": w["w3t"],
                                   "w2ht": w2ht} for c in range(NCORE)])
        return [res[c]["xout"] for c in range(NCORE)]

    def mlstm(xs):
        res = run_launch(km.nc, [{"xin": xs[c], "wqkT": w["wqkT"], "wvT": w["wvT"],
                                  "wogT": w["wogT"], "wifT": w["wifT"],
                                  "ifbias": w["ifbias"],
                                  "gmt": np.tile(w["gamma"][c][None, :], (128, 1)),
                                  "woutt": w["woutt0"], "i128": w["i128"],
                                  "i128f": w["i128f"], "trimp": w["trimp"]}
                                 for c in range(NCORE)])
        return [res[c]["xout"] for c in range(NCORE)]

    def ffn1(xs):
        res = run_launch(kf1.nc, [{"xin": xs[c], "wgt": w["wgt"], "wut": w["wut"],
                                   "wdt": w["wdt"], "sg": w["sg"], "su": w["su"],
                                   "sd": w["sd"]} for c in range(NCORE)])
        return [res[c]["xout"] for c in range(NCORE)]

    def attn(xs):
        res = run_launch(ka.nc, [{"xin": xs[c], "wqkvt": w["wqkvt"],
                                  "woutt": w["woutt1"], "i128": w["i128"],
                                  "trim": w["trim"]} for c in range(NCORE)])
        return [res[c]["xout"] for c in range(NCORE)]

    xs = ffn0(xs)
    xs = mlstm(xs)
    xs = ffn0(xs)
    xs = ffn1(xs)
    xs = attn(xs)
    xs = ffn1(xs)
    out = np.concatenate([xs[c].T for c in range(NCORE)], axis=0)
    return np.ascontiguousarray(out.reshape(B, S, D).astype(np.float32))
